# version 14
# baseline (speedup 1.0000x reference)
import contextlib
import numpy as np
import ml_dtypes
import concourse.bass as bass
import concourse.mybir as mybir
from concourse.bass_utils import run_bass_kernel_spmd

F32 = mybir.dt.float32
BF16 = mybir.dt.bfloat16
AF = mybir.ActivationFunctionType
ALU = mybir.AluOpType
AX = mybir.AxisListType
bfdt = ml_dtypes.bfloat16

D = 1024
T = 2048
FC = 8
H = 16
DK = 64
CH = 64
NCH = T // CH
E = 8
HM = 2048
DFF = 4096
NCORES = 8

ENGINES = ("sync", "gpsimd", "tensor", "vector", "scalar")


def _blk(w):
    r, c = w.shape
    return np.ascontiguousarray(w.reshape(r // 128, 128, c))


class Arena:

    def __init__(self, tensor, ncols):
        self.t = tensor
        self.ncols = ncols
        self.top = 0
        self.peak = 0

    def alloc(self, cols):
        cols = (cols + 15) & ~15
        off = self.top
        self.top += cols
        self.peak = max(self.peak, self.top)
        assert self.top <= self.ncols, f"arena overflow: {self.top} > {self.ncols}"
        return off

    def bf(self, off, cols, parts=128):
        return self.t[0:parts, off:off + cols]

    def f32(self, off, cols, parts=128):
        return self.t[0:parts, off:off + 2 * cols].bitcast(F32)

    def mark(self):
        return self.top

    def release(self, mark):
        self.top = mark


DMA_SEMS = ("c", "x", "x0", "x1", "st", "s0", "s1",
            "w0", "w1", "w2", "w3", "a0", "a1", "a2", "a3")


class Plan:
    def __init__(self, nc, ctx):
        self.nc = nc
        self.streams = {e: [] for e in ENGINES}
        self.sem = {}
        self.cnt = {}
        for name in DMA_SEMS + ("tS", "vS", "sS", "bar"):
            self.sem[name] = ctx.enter_context(nc.semaphore(name))
            self.cnt[name] = 0
        self._wrot = 0

    def next_wsem(self):
        s = "w" + str(self._wrot % 4)
        self._wrot += 1
        return s

    def em(self, eng, fn):
        self.streams[eng].append(fn)

    def dma(self, queue, out_ap, in_ap, sem, accum=False):
        self.cnt[sem] += 16
        val = self.cnt[sem]
        s = self.sem[sem]
        if accum:
            self.em(queue, lambda e, o=out_ap, i=in_ap, s_=s: e.dma_start(
                out=o, in_=i, accum_op=ALU.add).then_inc(s_, 16))
        else:
            self.em(queue, lambda e, o=out_ap, i=in_ap, s_=s: e.dma_start(
                out=o, in_=i).then_inc(s_, 16))
        return (sem, val)

    def wait(self, eng, key, val=None):
        if val is None:
            key, val = key
        if val > 0:
            self.em(eng, lambda e, s=self.sem[key], v=val: e.wait_ge(s, v))

    def tick(self, eng, key, instr_fn):
        self.cnt[key] += 1
        val = self.cnt[key]
        self.em(eng, lambda e, f=instr_fn, s=self.sem[key]: f(e).then_inc(s, 1))
        return val

    def inc(self, eng, key):
        self.cnt[key] += 1
        val = self.cnt[key]
        if eng == "tensor":
            self.em(eng, lambda e: e.drain())
        self.em(eng, lambda e, s=self.sem[key], v=val: e.sem_inc(s, 1))
        return val

    def barrier(self):
        for s in ("c", "x", "x0", "x1", "st", "s0", "s1"):
            self.wait("sync", s, self.cnt[s])
        for s in ("w0", "w1", "w2", "w3", "a0", "a1", "a2", "a3"):
            self.wait("gpsimd", s, self.cnt[s])
        self.cnt["bar"] += 5
        tgt = self.cnt["bar"]
        self.em("tensor", lambda e: e.drain())
        for eng in ENGINES:
            self.em(eng, lambda e, s=self.sem["bar"]: e.sem_inc(s, 1))
        for eng in ENGINES:
            self.em(eng, lambda e, s=self.sem["bar"], v=tgt: e.wait_ge(s, v))


def build(stop_phase=99, taps=(), l0_stage=3, l0_pairs=FC, l0_kinds=3, l0_ops=7):
    nc = bass.Bass("TRN2", target_bir_lowering=False, detect_race_conditions=False)
    taps = set(taps)
    dbg = {}

    dr = {}
    dr["xin"] = nc.dram_tensor("xin", [D, T], F32, kind="ExternalInput").ap()
    for nm, shp in [
        ("wq", [FC, 128, D]), ("wk", [FC, 128, D]), ("wv", [FC, 128, D]),
        ("wo", [FC, 128, D]),
        ("rwt", [FC, 128, E]), ("rbias", [1, E]),
        ("sw1", [FC, 128, HM]), ("sw2", [HM // 128, 128, D]),
        ("rw1", [E, FC, 128, HM]), ("rw2", [E, HM // 128, 128, D]),
        ("rb1", [128, E * 16]), ("rb2", [8, D]),
        ("sb1", [128, 16]), ("sb2", [1, D]),
        ("wa", [FC, FC, 128, 384]), ("wproj", [FC, 128, D]),
        ("w1t", [FC, 128, DFF]), ("w2t", [DFF // 128, 128, D]),
        ("b1", [128, 32]), ("b2", [128, 8]),
    ]:
        dr[nm] = nc.dram_tensor(nm, shp, F32, kind="ExternalInput").ap()
    out = nc.dram_tensor("out", [D, T], F32, kind="ExternalOutput").ap()
    xscr = nc.dram_tensor("xscr", [D, T], F32, kind="Internal").ap()
    xnscr = nc.dram_tensor("xnscr", [D, T], BF16, kind="Internal").ap()
    x1snap = nc.dram_tensor("x1snap", [D, T], F32, kind="Internal").ap()
    x2snap = nc.dram_tensor("x2snap", [D, T], F32, kind="Internal").ap()

    c_ident = nc.inline_tensor(np.eye(128, dtype=bfdt), "c_ident").ap()
    c_onescol = nc.inline_tensor(np.ones((128, 1), np.float32), "c_onescol").ap()
    c_sel1f = nc.inline_tensor(np.ones((1, 128), np.float32), "c_sel1f").ap()
    c_sel1b = nc.inline_tensor(np.ones((1, 128), bfdt), "c_sel1b").ap()
    c_onesrow = nc.inline_tensor(np.ones((1, 512), bfdt), "c_onesrow").ap()
    m0 = np.triu(np.ones((128, 128), np.float32))
    m0d = np.concatenate([m0, m0], axis=1).astype(bfdt)
    c_mask0 = nc.inline_tensor(m0d, "c_mask0").ap()
    ii = np.arange(128)[:, None]
    jj = np.arange(512)[None, :]
    m1 = np.stack([(128 * r + ii <= jj) for r in range(4)]).astype(bfdt)
    c_mask1 = nc.inline_tensor(np.ascontiguousarray(m1), "c_mask1").ap()
    mneg = np.ascontiguousarray(((m1.astype(np.float32) - 1.0) * 30000.0))
    c_maskneg = nc.inline_tensor(mneg, "c_maskneg").ap()
    c_onesall = nc.inline_tensor(np.ones((128, 128), bfdt), "c_onesall").ap()
    s8 = np.zeros((E, E, 128), np.float32)
    for e_ in range(E):
        s8[e_, e_, :] = 1.0
    c_sel8 = nc.inline_tensor(s8.astype(bfdt), "c_sel8").ap()

    with contextlib.ExitStack() as ctx:
        P = Plan(nc, ctx)
        sbt = lambda name, shape, dt: ctx.enter_context(nc.sbuf_tensor(name, shape, dt))

        ident = sbt("ident", [128, 128], BF16)
        onescol = sbt("onescol", [128, 1], F32)
        sel1f = sbt("sel1f", [1, 128], F32)
        sel1b = sbt("sel1b", [1, 128], BF16)
        onesrow = sbt("onesrow", [1, 512], BF16)
        mask0 = sbt("mask0", [128, 256], BF16)
        onesall = sbt("onesall", [128, 128], BF16)
        sel8 = sbt("sel8", [8, E * 128], BF16)
        b_rb1 = sbt("b_rb1", [128, E * 16], F32)
        b_sb1 = sbt("b_sb1", [128, 16], F32)
        b_b1 = sbt("b_b1", [128, 32], F32)
        b_b2 = sbt("b_b2", [128, 8], F32)
        rb2_sb = sbt("rb2_sb", [8, D], BF16)
        sb2_sb = sbt("sb2_sb", [1, D], BF16)
        rbias_sb = sbt("rbias_sb", [1, E], BF16)
        wT_sb = sbt("wT_sb", [8, T], BF16)
        m_row = sbt("m_row", [1, 512], F32)
        v_row = sbt("v_row", [1, 512], F32)
        r_row = sbt("r_row", [1, 512], F32)
        s_row = sbt("s_row", [1, 512], F32)
        rb_row = sbt("rb_row", [1, 512], BF16)
        sb_row = sbt("sb_row", [1, 512], BF16)
        t1_row = sbt("t1_row", [1, 512], F32)
        dr_row = sbt("dr_row", [1, 512], F32)
        drB_row = sbt("drB_row", [1, 512], F32)
        nrm_acc = sbt("nrm_acc", [1, 8], F32)
        negm_sb = sbt("negm_sb", [128, 1], F32)
        thr_sb = sbt("thr_sb", [128, 1], F32)
        thr2_sb = sbt("thr2_sb", [128, 1], F32)
        gp2 = sbt("gp2", [128, 64], F32)
        g_row = sbt("g_row", [1, 8], F32)

        pst = [ctx.enter_context(nc.psum_tensor(f"pst{i}", [128, 512], F32))
               for i in range(8)]
        pbf = [p[:, 0:512].bitcast(BF16) for p in pst]

        ARENA_COLS = 88576
        arena_t = sbt("arena", [128, ARENA_COLS], BF16)
        A = Arena(arena_t, ARENA_COLS)

        def tap_sb(name, ap):
            if name in taps:
                dbg[name] = (nc.dram_tensor("dbg_" + name, list(ap.shape), ap.dtype,
                                            kind="ExternalOutput").ap(), ap, "sb")

        def tap_dram(name, dram_ap):
            if name in taps:
                dbg[name] = (nc.dram_tensor("dbg_" + name, list(dram_ap.shape),
                                            dram_ap.dtype, kind="ExternalOutput").ap(),
                             dram_ap, "dram")

        for dst, src in ((ident, c_ident), (onescol, c_onescol), (sel1f, c_sel1f),
                         (sel1b, c_sel1b), (onesrow, c_onesrow), (mask0, c_mask0)):
            P.dma("sync", dst[:], src, "c")
        P.dma("sync", onesall[:], c_onesall, "c")
        P.dma("sync", sel8[:].rearrange("p (e c) -> p e c", e=E),
              c_sel8.rearrange("e p c -> p e c"), "c")
        P.dma("sync", b_rb1[:], dr["rb1"], "c")
        P.dma("sync", b_sb1[:], dr["sb1"], "c")
        P.dma("sync", b_b1[:], dr["b1"], "c")
        cst_s = P.dma("sync", b_b2[:], dr["b2"], "c")
        for eng in ("tensor", "vector", "scalar", "gpsimd"):
            P.wait(eng, cst_s)

        def blk_cast(sb_ap_fn, dram_ap, nblk, cols, sem):
            v = None
            for b in range(nblk):
                v = P.dma("gpsimd", sb_ap_fn(b), dram_ap[b], sem)
            return v

        def emit_layernorm(src_dram, xn_view, store_xscr=False, store_xn=False,
                           snap_dram=None):
            mk = A.mark()
            xt_off = [A.alloc(1024) for _ in range(FC)]
            x2_off = [A.alloc(1024) for _ in range(2)]
            tmp_off = A.alloc(512)
            xt = [A.f32(o, 512) for o in xt_off]
            x2 = [A.f32(o, 512) for o in x2_off]
            tmp = A.bf(tmp_off, 512)
            if "lnstat" in taps and "lnstat" not in dbg:
                st_off = A.alloc(8192)
                lnstat = A.f32(st_off, 4096, parts=1)
                tap_sb("lnstat", lnstat)
            else:
                lnstat = None
            for tt in range(4):
                sl = slice(tt * 512, (tt + 1) * 512)
                ldgrp = None
                for fc in range(FC):
                    ldgrp = P.dma("sync", xt[fc], src_dram[fc * 128:(fc + 1) * 128, sl], "x")
                P.wait("tensor", ldgrp)
                P.wait("scalar", ldgrp)
                tstat = 0
                mm2_done = [0] * FC
                for fc in range(FC):
                    P.em("tensor", lambda e, f=fc: e.matmul(
                        pst[2][0:1, :], onescol[:], xt[f],
                        start=(f == 0), stop=(f == FC - 1), skip_group_check=True))
                    if fc >= 2:
                        P.wait("scalar", "tS", mm2_done[fc - 2])
                    sv = P.tick("scalar", "sS", lambda e, f=fc: e.activation(
                        x2[f % 2], xt[f], AF.Square))
                    P.wait("tensor", "sS", sv)
                    mm2 = lambda e, f=fc: e.matmul(
                        pst[3][0:1, :], onescol[:], x2[f % 2],
                        start=(f == 0), stop=(f == FC - 1), skip_group_check=True)
                    mm2_done[fc] = P.tick("tensor", "tS", mm2)
                tstat = mm2_done[FC - 1]
                P.wait("vector", "tS", tstat)
                P.em("vector", lambda e: e.tensor_scalar_mul(m_row[:], pst[2][0:1, :], 1.0 / D))
                P.em("vector", lambda e: e.tensor_scalar_mul(v_row[:], pst[3][0:1, :], 1.0 / D))
                P.em("vector", lambda e: e.tensor_tensor(t1_row[:], m_row[:], m_row[:], ALU.mult))
                P.em("vector", lambda e: e.tensor_tensor(v_row[:], v_row[:], t1_row[:], ALU.subtract))
                vv = P.tick("vector", "vS", lambda e: e.tensor_scalar_add(v_row[:], v_row[:], 1e-5))
                P.wait("scalar", "vS", vv)
                sv = P.tick("scalar", "sS", lambda e: e.activation(v_row[:], v_row[:], AF.Sqrt))
                P.wait("vector", "sS", sv)
                P.em("vector", lambda e: e.reciprocal(r_row[:], v_row[:]))
                if lnstat is not None:
                    P.em("vector", lambda e, t=tt: e.tensor_copy(
                        lnstat[0:1, t * 512:(t + 1) * 512], m_row[:]))
                    P.em("vector", lambda e, t=tt: e.tensor_copy(
                        lnstat[0:1, 2048 + t * 512:2048 + (t + 1) * 512], r_row[:]))
                P.em("vector", lambda e: e.tensor_tensor(s_row[:], m_row[:], r_row[:], ALU.mult))
                P.em("vector", lambda e: e.tensor_scalar_mul(s_row[:], s_row[:], -1.0))
                P.em("vector", lambda e: e.tensor_copy(rb_row[:], r_row[:]))
                vv = P.tick("vector", "vS", lambda e: e.tensor_copy(sb_row[:], s_row[:]))
                P.wait("tensor", "vS", vv)
                P.em("tensor", lambda e: e.matmul(pst[4][:, :], sel1b[:], rb_row[:],
                                                  start=True, stop=True, skip_group_check=True))
                trep = P.tick("tensor", "tS", lambda e: e.matmul(
                    pst[5][:, :], sel1b[:], sb_row[:], start=True, stop=True,
                    skip_group_check=True))
                P.wait("vector", "tS", trep)
                last_v = 0
                for fc in range(FC):
                    dst = xn_view(fc)[:, sl]
                    P.em("vector", lambda e, f=fc: e.tensor_tensor(
                        tmp, xt[f], pst[4][:, :], ALU.mult))
                    last_v = P.tick("vector", "vS", lambda e, d=dst, f=fc: e.tensor_tensor(
                        d, tmp, pst[5][:, :], ALU.add))
                    if store_xn:
                        P.wait("sync", "vS", last_v)
                        P.dma("sync", xnscr[fc * 128:(fc + 1) * 128, sl], dst, "st")
                    if store_xscr:
                        P.wait("sync", "vS", last_v)
                        P.dma("sync", xscr[fc * 128:(fc + 1) * 128, sl], xt[fc], "st")
                    if snap_dram is not None:
                        P.wait("sync", "vS", last_v)
                        P.dma("sync", snap_dram[fc * 128:(fc + 1) * 128, sl], xt[fc], "st")
                P.wait("tensor", "vS", last_v)
                P.wait("sync", "vS", last_v)
                P.wait("sync", "st", P.cnt["st"])
            A.release(mk)

        g_xn = A.alloc(FC * T)
        mkL0 = A.mark()
        g_wq = A.alloc(FC * D); g_wk = A.alloc(FC * D)
        g_wv = A.alloc(FC * D); g_wo = A.alloc(FC * D)
        g_rwt = A.alloc(FC * E)
        g_yT = A.alloc(FC * T)
        wq_sb = A.bf(g_wq, FC * D); wk_sb = A.bf(g_wk, FC * D)
        wv_sb = A.bf(g_wv, FC * D); wo_sb = A.bf(g_wo, FC * D)
        rwt_sb = A.bf(g_rwt, FC * E)
        xn = lambda fc: A.bf(g_xn + fc * T, T)
        yT = lambda fc: A.bf(g_yT + fc * T, T)

        wsem0 = P.next_wsem()
        P.dma("gpsimd", rb2_sb[:], dr["rb2"], wsem0)
        P.dma("gpsimd", sb2_sb[:], dr["sb2"], wsem0)
        P.dma("gpsimd", rbias_sb[:], dr["rbias"], wsem0)
        blk_cast(lambda b: wq_sb[:, b * D:(b + 1) * D], dr["wq"], FC, D, wsem0)
        blk_cast(lambda b: wk_sb[:, b * D:(b + 1) * D], dr["wk"], FC, D, wsem0)
        blk_cast(lambda b: wv_sb[:, b * D:(b + 1) * D], dr["wv"], FC, D, wsem0)
        blk_cast(lambda b: wo_sb[:, b * D:(b + 1) * D], dr["wo"], FC, D, wsem0)
        w0_done = blk_cast(lambda b: rwt_sb[:, b * E:(b + 1) * E], dr["rwt"], FC, E, wsem0)
        wq_done = wk_done = wv_done = wo_done = rwt_done = w0_done

        emit_layernorm(dr["xin"], xn, store_xscr=True)
        if "xn0" in taps:
            tap_sb("xn0", A.bf(g_xn, FC * T))
        P.barrier()

        if stop_phase >= 2:
            mk2 = A.mark()
            o_qf = A.alloc(T); o_kf = A.alloc(T); o_vf = A.alloc(T)
            o_kt = [A.alloc(1024) for _ in range(2)]
            o_vt = [A.alloc(1040) for _ in range(2)]
            o_st = A.alloc(80)
            o_at2 = [A.alloc(256), A.alloc(256)]
            o_num = A.alloc(1024)
            o_el1 = A.alloc(2048)
            o_el2 = A.alloc(1024)
            qf = A.bf(o_qf, T); kf = A.bf(o_kf, T); vf = A.bf(o_vf, T)
            kt = [A.bf(o, 1024) for o in o_kt]
            vt = [A.bf(o, 1040) for o in o_vt]
            stil = A.bf(o_st, 80)
            atil2 = [A.bf(o, 256) for o in o_at2]
            num_sb = A.f32(o_num, 512)
            el1p = [A.f32(o_el1, 512), A.f32(o_el1 + 1024, 512)]
            el2p = [A.bf(o_el2, 512), A.bf(o_el2 + 512, 512)]

            P.wait("tensor", wv_done)
            bank_free = {0: None, 1: None}
            for hp in range(l0_pairs):
                for kind in range(l0_kinds):
                    wsb = (wq_sb, wk_sb, wv_sb)[kind]
                    dst = (qf, kf, vf)[kind]
                    for tt in range(4):
                        bk = (kind * 4 + tt) % 2
                        if bank_free[bk] is not None:
                            P.wait("tensor", *bank_free[bk])
                        tv = 0
                        for fc in range(FC):
                            mm = lambda e, f=fc, b=bk, w=wsb, h=hp, t=tt: e.matmul(
                                pst[b][:, :], w[:, f * D + h * 128: f * D + h * 128 + 128],
                                xn(f)[:, t * 512:(t + 1) * 512],
                                start=(f == 0), stop=(f == FC - 1), skip_group_check=True)
                            if fc == FC - 1:
                                tv = P.tick("tensor", "tS", mm)
                            else:
                                P.em("tensor", mm)
                        sl = slice(tt * 512, (tt + 1) * 512)
                        if kind < 2:
                            ep = tt % 2
                            P.wait("vector", "tS", tv)
                            vv = P.tick("vector", "vS", lambda e, b=bk, p_=ep: e.tensor_single_scalar(
                                el1p[p_], pst[b][:, :], 0.0, ALU.min))
                            P.wait("scalar", "vS", vv)
                            sv = P.tick("scalar", "sS", lambda e, p_=ep: e.activation(
                                el2p[p_], el1p[p_], AF.Exp))
                            P.wait("vector", "sS", sv)
                            bank_free[bk] = ("vS", P.tick("vector", "vS",
                                lambda e, b=bk, d=dst, s=sl, p_=ep:
                                e.scalar_tensor_tensor(d[:, s], pst[b][:, :], 0.0, el2p[p_],
                                                       ALU.max, ALU.add)))
                        else:
                            P.wait("vector", "tS", tv)
                            sv = P.tick("vector", "vS", lambda e, b=bk, d=dst, s=sl:
                                e.tensor_copy(d[:, s], pst[b][:, :]))
                            bank_free[bk] = ("vS", sv)
                vready = P.inc("vector", "vS")
                sready = P.inc("scalar", "sS")
                P.wait("tensor", "vS", vready)
                P.wait("tensor", "sS", sready)
                if l0_stage < 2:
                    continue
                for i in range(2):
                    idsl = ident[64 * i:64 * i + 64, 64 * i:64 * i + 64]
                    for tc in range(16):
                        tr = lambda e, i_=i, t=tc, s=idsl: e.transpose(
                            pbf[6][:, t * 64:t * 64 + 64],
                            kf[64 * i_:64 * i_ + 64, t * 128:(t + 1) * 128], s)
                        if tc == 15:
                            tv = P.tick("tensor", "tS", tr)
                        else:
                            P.em("tensor", tr)
                    P.wait("scalar", "tS", tv)
                    skt = P.tick("scalar", "sS", lambda e, i_=i: e.copy(kt[i_][:, :], pbf[6][:, 0:1024]))
                    for tc in range(16):
                        tr = lambda e, i_=i, t=tc, s=idsl: e.transpose(
                            pbf[7][:, t * 64:t * 64 + 64],
                            vf[64 * i_:64 * i_ + 64, t * 128:(t + 1) * 128], s)
                        if tc == 15:
                            tv = P.tick("tensor", "tS", tr)
                        else:
                            P.em("tensor", tr)
                    P.wait("vector", "tS", tv)
                    vvt = P.tick("vector", "vS", lambda e, i_=i: e.tensor_copy(
                        vt[i_][:].rearrange("p (t c) -> p t c", c=65)[:, :, 0:64],
                        pbf[7][:, 0:1024].rearrange("p (t c) -> p t c", c=64)))
                    P.em("vector", lambda e, i_=i: e.memset(
                        vt[i_][:].rearrange("p (t c) -> p t c", c=65)[:, :, 64:65], 1.0))
                    P.wait("tensor", "sS", skt)
                    P.wait("tensor", "vS", vvt)
                if l0_stage < 3:
                    continue
                vz = P.tick("vector", "vS", lambda e: e.memset(stil[:, :], 0.0))
                P.wait("tensor", "vS", vz)
                s2_last = 0
                vmask_done = [0, 0]
                pend_intra = None

                def emit_intra(n_, vm, last):
                    ap_ = n_ % 2
                    cc_ = slice(128 * (n_ % 4), 128 * (n_ % 4) + 128)
                    P.wait("tensor", "vS", vm)
                    P.em("tensor", lambda e, c=cc_, a=atil2[ap_]: e.matmul(
                        pst[3][0:65, c], vt[0][:, n_ * 65:n_ * 65 + 65],
                        a[:, 0:128], start=False, stop=True, skip_group_check=True))
                    P.em("tensor", lambda e, c=cc_, a=atil2[ap_]: e.matmul(
                        pst[4][64:128, c], vt[1][:, n_ * 65:n_ * 65 + 64],
                        a[:, 128:256], start=False, stop=True, skip_group_check=True))
                    mmz = lambda e, c=cc_, a=atil2[ap_]: e.matmul(
                        pst[4][0:1, c], vt[1][:, n_ * 65 + 64:n_ * 65 + 65],
                        a[:, 128:256], start=False, stop=True, skip_group_check=True)
                    if last:
                        return P.tick("tensor", "tS", mmz)
                    P.em("tensor", mmz)
                    return 0

                for cg in range(4):
                    for n4 in range(4):
                        n = cg * 4 + n4
                        abank = pst[n % 2]
                        nsl = slice(n * 128, (n + 1) * 128)
                        bbank = pst[6 + n % 2]
                        P.wait("tensor", "vS", vmask_done[n % 2])
                        for i in range(2):
                            mm = lambda e, i_=i, s=nsl, b=(abank if i == 0 else bbank): \
                                e.matmul(b[:, 0:128],
                                         kf[64 * i_:64 * i_ + 64, s],
                                         qf[64 * i_:64 * i_ + 64, s],
                                         start=True, stop=True, skip_group_check=True)
                            if i == 1:
                                t_attn = P.tick("tensor", "tS", mm)
                            else:
                                P.em("tensor", mm)
                        cc = slice(128 * n4, 128 * n4 + 128)
                        P.wait("tensor", "sS", s2_last)
                        P.em("tensor", lambda e, s=nsl, c=cc, st=(n4 == 0): e.matmul(
                            pst[3][0:65, c], stil[0:64, 0:65], qf[0:64, s],
                            start=st, stop=False, skip_group_check=True))
                        P.em("tensor", lambda e, s=nsl, c=cc, st=(n4 == 0): e.matmul(
                            pst[4][64:128, c], stil[64:128, 0:64], qf[64:128, s],
                            start=st, stop=False, skip_group_check=True))
                        P.em("tensor", lambda e, s=nsl, c=cc, st=(n4 == 0): e.matmul(
                            pst[4][0:1, c], stil[64:128, 64:65], qf[64:128, s],
                            start=st, stop=False, skip_group_check=True))
                        for i in range(2):
                            mm = lambda e, i_=i, n_=n: e.matmul(
                                pst[2][64 * i_:64 * i_ + 64, 0:65],
                                kt[i_][:, n_ * 64:(n_ + 1) * 64],
                                vt[i_][:, n_ * 65:n_ * 65 + 65],
                                start=(n_ == 0), stop=(n_ == 15),
                                skip_group_check=True)
                            if i == 1:
                                t_kv = P.tick("tensor", "tS", mm)
                            else:
                                P.em("tensor", mm)
                        if pend_intra is not None:
                            emit_intra(pend_intra[0], pend_intra[1], False)
                            pend_intra = None
                        P.wait("vector", "tS", t_attn)
                        P.em("vector", lambda e, b=abank, a=atil2[n % 2]: e.tensor_tensor(
                            a[:, 0:128], b[:, 0:128], mask0[:, 0:128], ALU.mult))
                        vmask_done[n % 2] = P.tick("vector", "vS",
                            lambda e, b=bbank, a=atil2[n % 2]: e.tensor_tensor(
                                a[:, 128:256], b[:, 0:128], mask0[:, 128:256], ALU.mult))
                        P.wait("scalar", "tS", t_kv)
                        s2_last = P.tick("scalar", "sS", lambda e: e.copy(
                            stil[:, 0:65], pst[2][:, 0:65]))
                        pend_intra = (n, vmask_done[n % 2])
                    t_cg = emit_intra(pend_intra[0], pend_intra[1], True)
                    pend_intra = None
                    if False:
                        pass
                    if not (l0_ops & 4):
                        continue
                    P.wait("vector", "tS", t_cg)
                    P.em("vector", lambda e: e.tensor_single_scalar(
                        dr_row[:], pst[3][64:65, :], 1e-6, ALU.max))
                    P.em("vector", lambda e: e.reciprocal(dr_row[:], dr_row[:]))
                    P.em("vector", lambda e: e.tensor_single_scalar(
                        drB_row[:], pst[4][0:1, :], 1e-6, ALU.max))
                    v_dr = P.tick("vector", "vS", lambda e: e.reciprocal(drB_row[:], drB_row[:]))
                    P.wait("tensor", "vS", v_dr)
                    P.em("tensor", lambda e: e.matmul(pst[5][0:64, :], sel1f[0:1, 0:64],
                                                      dr_row[:], start=True, stop=True,
                                                      skip_group_check=True))
                    t_rep = P.tick("tensor", "tS", lambda e: e.matmul(
                        pst[5][64:128, :], sel1f[0:1, 0:64], drB_row[:],
                        start=True, stop=True, skip_group_check=True))
                    P.wait("scalar", "tS", t_cg)
                    P.em("scalar", lambda e: e.copy(num_sb[0:64, :], pst[3][0:64, :]))
                    s_num = P.tick("scalar", "sS", lambda e: e.copy(
                        num_sb[64:128, :], pst[4][64:128, :]))
                    P.wait("vector", "sS", s_num)
                    P.wait("vector", "tS", t_rep)
                    yv = P.tick("vector", "vS", lambda e, h=hp, c=cg: e.tensor_tensor(
                        yT(h)[:, c * 512:(c + 1) * 512], num_sb, pst[5][:, :], ALU.mult))
                    P.wait("tensor", "sS", s_num)
                    P.wait("tensor", "vS", yv)
            if "y0" in taps:
                tap_sb("y0", A.bf(g_yT, FC * T))
            t_p2_end = P.cnt["tS"]
            A.release(mk2)
            P.barrier()

        if stop_phase >= 3:
            mk = A.mark()
            o_lg = A.alloc(32)
            lg = A.f32(o_lg, 8)
            pn = A.f32(A.alloc(32), 8)
            ind = A.f32(A.alloc(32), 8)
            m8 = A.f32(A.alloc(32), 8)
            nmx = A.f32(A.alloc(8), 1)
            smx = A.f32(A.alloc(8), 1)
            wtok = A.bf(A.alloc(16), 8)
            smxr = A.f32(A.alloc(8), 1)
            spacer8 = A.f32(A.alloc(32), 8)
            P.wait("tensor", rwt_done)
            P.wait("tensor", "vS", P.cnt["vS"])
            sc_last = 0
            for tc in range(16):
                for fc in range(FC):
                    P.em("tensor", lambda e, f=fc, t=tc: e.matmul(
                        pst[6][:, 8 * t:8 * t + 8],
                        xn(f)[:, t * 128:(t + 1) * 128],
                        rwt_sb[:, f * E:(f + 1) * E],
                        start=(f == 0), stop=False, skip_group_check=True))
                P.em("tensor", lambda e: e.drain())
                mmb = lambda e, t=tc: e.matmul(
                    pst[6][:, 8 * t:8 * t + 8], sel1b[:], rbias_sb[:],
                    start=False, stop=True, skip_group_check=True)
                if tc == 15:
                    t_rl = P.tick("tensor", "tS", mmb)
                else:
                    P.em("tensor", mmb)
            P.wait("vector", "tS", t_rl)
            P.wait("scalar", "tS", t_rl)
            wtok_free = 0
            tr_done = []
            for tc in range(16):
                vv = P.tick("vector", "vS", lambda e, t=tc: e.reduce_max(
                    nmx, pst[6][:, 8 * t:8 * t + 8], axis=AX.X, negate=True))
                P.wait("scalar", "vS", vv)
                sv = P.tick("scalar", "sS", lambda e, t=tc: e.activation(
                    lg, pst[6][:, 8 * t:8 * t + 8], AF.Exp, bias=nmx, accum_out=smx))
                P.wait("vector", "sS", sv)
                vv = P.tick("vector", "vS", lambda e: e.reciprocal(smxr, smx))
                P.wait("scalar", "vS", vv)
                sv = P.tick("scalar", "sS", lambda e: e.activation(
                    pn, lg, AF.Copy, scale=smxr))
                P.wait("vector", "sS", sv)
                P.em("vector", lambda e: e.max(out=m8, in_=pn))
                P.em("vector", lambda e: e.tensor_copy(spacer8, pn))
                P.em("vector", lambda e: e.tensor_single_scalar(ind, pn, m8[0:128, 1:2], ALU.is_ge))
                P.em("vector", lambda e: e.tensor_copy(spacer8, pn))
                P.wait("vector", "tS", wtok_free)
                vv = P.tick("vector", "vS", lambda e: e.tensor_tensor(wtok, pn, ind, ALU.mult))
                P.wait("tensor", "vS", vv)
                tv = P.tick("tensor", "tS", lambda e, t=tc: e.transpose(
                    pbf[7][0:8, (t % 8) * 128:(t % 8) * 128 + 128], wtok, ident[:]))
                wtok_free = tv
                tr_done.append(tv)
                if tc % 8 == 7:
                    P.wait("scalar", "tS", tv)
                    sc_last = P.tick("scalar", "sS", lambda e, t=tc: e.copy(
                        wT_sb[0:8, (t - 7) * 128:(t + 1) * 128], pbf[7][0:8, 0:1024]))
                    P.wait("tensor", "sS", sc_last)
            tap_sb("r_lg", lg); tap_sb("r_pn", pn); tap_sb("r_ind", ind)
            tap_sb("r_m8", m8); tap_sb("r_smx", smx); tap_sb("r_nmx", nmx)
            tap_sb("r_wtok", wtok)
            A.release(mk)
            if "wtop" in taps:
                tap_sb("wtop", wT_sb[:])
            P.barrier()

            mk = A.mark()
            osb = [A.f32(A.alloc(1024), 512) for _ in range(2)]
            osb_free = [("a0", 0), ("a1", 0)]
            P.wait("tensor", wo_done)
            for oc in range(FC):
                for tt in range(4):
                    bk = (oc * 4 + tt) % 2
                    for fc in range(FC):
                        P.em("tensor", lambda e, f=fc, o=oc, t=tt, b=bk: e.matmul(
                            pst[b][:, :], wo_sb[:, f * D + o * 128: f * D + o * 128 + 128],
                            yT(f)[:, t * 512:(t + 1) * 512],
                            start=(f == 0), stop=False, skip_group_check=True))
                    P.em("tensor", lambda e: e.drain())
                    P.em("tensor", lambda e, o=oc, t=tt, b=bk: e.matmul(
                        pst[b][:, :], rb2_sb[0:8, o * 128:o * 128 + 128],
                        wT_sb[0:8, t * 512:(t + 1) * 512],
                        start=False, stop=False, skip_group_check=True))
                    tv = P.tick("tensor", "tS", lambda e, o=oc, b=bk: e.matmul(
                        pst[b][:, :], sb2_sb[0:1, o * 128:o * 128 + 128],
                        onesrow[:], start=False, stop=True, skip_group_check=True))
                    P.wait("scalar", "tS", tv)
                    P.wait("scalar", osb_free[bk])
                    sv = P.tick("scalar", "sS", lambda e, b=bk: e.copy(osb[b], pst[b][:, :]))
                    P.wait("tensor", "sS", sv)
                    P.wait("gpsimd", "sS", sv)
                    osb_free[bk] = P.dma(
                        "gpsimd", xscr[oc * 128:(oc + 1) * 128, tt * 512:(tt + 1) * 512],
                        osb[bk], "a" + str(bk), accum=True)
            t_p3_end = P.cnt["tS"]
            A.release(mk)
            tap_dram("xscr3", xscr)
            if "x2snap" in taps:
                mkS = A.mark()
                snapb = A.f32(A.alloc(4096), 2048)
                for fc in range(FC):
                    lvs = P.dma("sync", snapb, xscr[fc * 128:(fc + 1) * 128, :], "x")
                    P.wait("sync", lvs)
                    svs = P.dma("sync", x2snap[fc * 128:(fc + 1) * 128, :], snapb, "st")
                    P.wait("sync", svs)
                A.release(mkS)
                tap_dram("x2snap", x2snap)
            P.barrier()

        if stop_phase >= 4:
            A.release(mkL0)
            mk = A.mark()
            o_wb = [A.alloc(16384) for _ in range(3)]
            wb = [A.bf(o, 16384) for o in o_wb]
            o_hb = A.alloc(FC * 1024)
            hbuf = A.bf(o_hb, FC * 1024)
            hg = A.bf(A.alloc(1024), 512)
            hg2 = A.bf(A.alloc(1024), 512)
            dsum = [A.f32(A.alloc(1024), 512) for _ in range(4)]
            dsum_free = [("a0", 0), ("a1", 0), ("a2", 0), ("a3", 0)]
            bu = lambda k: [0, 2, 1][k % 3]
            bd = lambda k: [1, 0, 2][k % 3]
            up_done = [None] * 9
            dn_done = [None] * 9
            P.wait("gpsimd", "tS", t_p2_end)
            up_done[0] = blk_cast(lambda b: wb[0][:, b * HM:(b + 1) * HM], dr["sw1"],
                                  FC, HM, P.next_wsem())
            P.wait("gpsimd", "tS", t_p3_end)
            dn_done[0] = blk_cast(lambda b: wb[1][:, b * D:(b + 1) * D], dr["sw2"],
                                  16, D, P.next_wsem())
            up_done[1] = blk_cast(lambda b: wb[2][:, b * HM:(b + 1) * HM], dr["rw1"][0],
                                  FC, HM, P.next_wsem())
            accum_prev = None
            for k in range(9):
                ubuf, dbuf = wb[bu(k)], wb[bd(k)]
                shared = (k == 0)
                e_ = k - 1
                P.wait("tensor", up_done[k])
                wrep_free = 0
                for tt in range(4):
                    tsl = slice(tt * 512, (tt + 1) * 512)
                    t_wrep = 0
                    if not shared:
                        P.wait("tensor", "vS", wrep_free)
                        t_wrep = P.tick("tensor", "tS", lambda e, t=tt, x=e_: e.matmul(
                            pst[2][:, :], sel8[0:8, x * 128:(x + 1) * 128],
                            wT_sb[0:8, t * 512:(t + 1) * 512],
                            start=True, stop=True, skip_group_check=True))
                    ev_last = 0
                    ev_bank = [None, None]
                    hg_free = [0, 0]
                    for hc in range(16):
                        bk = hc % 2
                        if ev_bank[bk] is not None:
                            P.wait("tensor", *ev_bank[bk])
                        tv = 0
                        for fc in range(FC):
                            mm = lambda e, f=fc, h=hc, t=tsl, b=bk, u=ubuf: e.matmul(
                                pst[b][:, :], u[:, f * HM + h * 128: f * HM + h * 128 + 128],
                                xn(f)[:, t], start=(f == 0), stop=(f == FC - 1),
                                skip_group_check=True)
                            if fc == FC - 1:
                                tv = P.tick("tensor", "tS", mm)
                            else:
                                P.em("tensor", mm)
                        P.wait("scalar", "tS", tv)
                        bias_ap = (b_sb1[:, hc:hc + 1] if shared
                                   else b_rb1[:, e_ * 16 + hc: e_ * 16 + hc + 1])
                        if shared:
                            ev_last = P.tick("scalar", "sS", lambda e, h=hc, b=bk, ba=bias_ap:
                                e.activation(hbuf[:, h * 512:(h + 1) * 512], pst[b][:, :],
                                             AF.Gelu, bias=ba))
                            ev_bank[bk] = ("sS", ev_last)
                        else:
                            g_ = hg if bk == 0 else hg2
                            P.wait("scalar", "vS", hg_free[bk])
                            sv = P.tick("scalar", "sS", lambda e, b=bk, ba=bias_ap, g=g_:
                                e.activation(g, pst[b][:, :], AF.Gelu, bias=ba))
                            P.wait("vector", "sS", sv)
                            P.wait("vector", "tS", t_wrep)
                            ev_last = P.tick("vector", "vS", lambda e, h=hc, g=g_:
                                e.tensor_tensor(hbuf[:, h * 512:(h + 1) * 512], g,
                                                pst[2][:, :], ALU.mult))
                            hg_free[bk] = ev_last
                            ev_bank[bk] = ("sS", sv)
                    wrep_free = ev_last if not shared else 0
                    if tt == 3:
                        if k < 8:
                            P.wait("gpsimd", "tS", tv)
                            dn_done[k + 1] = blk_cast(
                                lambda b, bb=bu(k): wb[bb][:, b * D:(b + 1) * D],
                                dr["rw2"][k], 16, D, P.next_wsem())
                    P.wait("tensor", "sS" if shared else "vS", ev_last)
                    P.wait("tensor", dn_done[k])
                    if tt == 0 and accum_prev is not None:
                        for s_, v_ in accum_prev.items():
                            P.wait("gpsimd", s_, v_)
                    for dch in range(2):
                        tdn = 0
                        for hc in range(16):
                            for dc4 in range(4):
                                dc = dch * 4 + dc4
                                mm = lambda e, h=hc, d=dc, b=3 + dc4, t=tsl, db=dbuf: \
                                    e.matmul(pst[b][:, :],
                                             db[:, h * D + d * 128: h * D + d * 128 + 128],
                                             hbuf[:, h * 512:(h + 1) * 512],
                                             start=(h == 0), stop=(h == 15),
                                             skip_group_check=True)
                                if hc == 15 and dc4 == 3:
                                    tdn = P.tick("tensor", "tS", mm)
                                else:
                                    P.em("tensor", mm)
                        P.wait("vector", "tS", tdn)
                        for dc4 in range(4):
                            dc = dch * 4 + dc4
                            P.wait("vector", dsum_free[dc4])
                            vv = P.tick("vector", "vS", lambda e, b=3 + dc4, d=dc4:
                                e.tensor_copy(dsum[d], pst[b][:, :]))
                            P.wait("gpsimd", "vS", vv)
                            dsum_free[dc4] = P.dma(
                                "gpsimd",
                                xscr[dc * 128:(dc + 1) * 128, tsl], dsum[dc4],
                                "a" + str(dc4), accum=True)
                            P.wait("tensor", "vS", vv)
                accum_prev = {a: P.cnt[a] for a in ("a0", "a1", "a2", "a3")}
                if k < 7:
                    P.wait("gpsimd", "tS", tdn)
                    up_done[k + 2] = blk_cast(
                        lambda b, bb=bd(k): wb[bb][:, b * HM:(b + 1) * HM],
                        dr["rw1"][k + 1], FC, HM, P.next_wsem())
            A.release(mk)
            tap_dram("xscr4", xscr)
            P.barrier()

        if stop_phase >= 5:
            A.release(g_xn)
            g_xn2 = A.alloc(FC * T)
            xn2 = lambda fc: A.bf(g_xn2 + fc * T, T)
            emit_layernorm(xscr, xn2, store_xn=True, snap_dram=x1snap)
            tap_dram("x1snap", x1snap)
            if "xn1" in taps:
                tap_sb("xn1", A.bf(g_xn2, FC * T))
            P.barrier()

        if stop_phase >= 6:
            g_yT2 = A.alloc(FC * T)
            yT2 = lambda fc: A.bf(g_yT2 + fc * T, T)
            o_wa = [A.alloc(3072), A.alloc(3072)]
            wa_sb = [A.bf(o, 3072) for o in o_wa]
            o_wpj = A.alloc(FC * D)
            wpj_sb = A.bf(o_wpj, FC * D)
            mk6 = A.mark()
            qT6 = [[A.bf(A.alloc(T), T) for _ in range(2)] for _ in range(2)]
            kT6 = [[A.bf(A.alloc(T), T) for _ in range(2)] for _ in range(2)]
            vT6 = [A.bf(A.alloc(T), T) for _ in range(2)]
            vt6 = [[A.bf(A.alloc(1040), 1040) for _ in range(2)] for _ in range(2)]
            passA_scr = A.f32(A.alloc(1024), 512)
            mask1 = A.bf(A.alloc(4 * 512), 4 * 512)
            maskneg = A.f32(A.alloc(8 * 512), 4 * 512)
            mld = P.dma("sync", mask1.rearrange("p (r c) -> p r c", r=4),
                        c_mask1.rearrange("r p c -> p r c"), "c")
            mld = P.dma("sync", maskneg.rearrange("p (r c) -> p r c", r=4),
                        c_maskneg.rearrange("r p c -> p r c"), "c")
            P.wait("vector", mld)
            for par_ in range(2):
                for i_ in range(2):
                    P.em("vector", lambda e, p_=par_, j_=i_: e.memset(
                        kT6[p_][j_][64:65, :], 1.0))
            stg6 = [A.bf(A.alloc(512), 512) for _ in range(2)]
            stg6_gate = [None, None]
            o_pp = [A.alloc(512) for _ in range(4)]
            pp = [A.bf(o, 512) for o in o_pp]
            o_num6 = A.alloc(1024)
            num6 = A.f32(o_num6, 512)
            mq_col = A.f32(A.alloc(16), 1)
            mq_mat = A.f32(A.alloc(16), 4)
            spacer6 = A.f32(A.alloc(32), 8)
            mq_tmp = A.f32(A.alloc(16), 1)
            mqn_bf = A.bf(A.alloc(16), 1)
            mqn8 = A.bf(A.alloc(16), 8)

            def wa_dma(hp, slot):
                return P.dma("gpsimd",
                             wa_sb[slot][:].rearrange("p (b c) -> p b c", b=FC),
                             dr["wa"][hp].rearrange("b p c -> p b c"), P.next_wsem())
            wa_done = [None] * 10
            P.wait("gpsimd", "tS", P.cnt["tS"])
            P.wait("gpsimd", "vS", P.cnt["vS"])
            P.wait("gpsimd", "st", P.cnt["st"])
            wa_done[0] = wa_dma(0, 0)
            wa_done[1] = wa_dma(1, 1)
            wpj_done = blk_cast(lambda b: wpj_sb[:, b * D:(b + 1) * D], dr["wproj"],
                                FC, D, P.next_wsem())

            s1_state = [None, None]

            def make_stage1(hp):
                par = hp % 2
                slot = hp % 2
                info = {"proj": 0, "vt": 0, "mrow": 0, "tmq": 0}
                s1_state[par] = info
                bankp = [2]
                bank_gate = {2: None, 6: None}
                slot_ev = [0] * 8

                def nb():
                    b = bankp[0]
                    bankp[0] = 8 - b
                    return b

                def proj_chunk(part, tt):
                    def em():
                        b = nb()
                        if bank_gate[b] is not None:
                            for h_ in bank_gate[b]:
                                P.wait("tensor", *h_)
                        if part == 0 and tt == 0:
                            P.wait("tensor", wa_done[hp])
                        tv = 0
                        for fc in range(FC):
                            mm = lambda e, f=fc, b_=b, s=slot, pt=part, t=tt: e.matmul(
                                pst[b_][:, :],
                                wa_sb[s][:, f * 384 + pt * 128: f * 384 + pt * 128 + 128],
                                xn2(f)[:, t * 512:(t + 1) * 512],
                                start=(f == 0), stop=(f == FC - 1), skip_group_check=True)
                            if fc == FC - 1:
                                tv = P.tick("tensor", "tS", mm)
                            else:
                                P.em("tensor", mm)
                        P.wait("scalar", "tS", tv)
                        if part == 2:
                            sv = P.tick("scalar", "sS", lambda e, b_=b, t=tt: e.copy(
                                vT6[par][:, t * 512:(t + 1) * 512], pst[b_][:, :]))
                            bank_gate[b] = [("sS", sv)]
                        else:
                            d2 = (qT6, kT6)[part][par]
                            P.em("scalar", lambda e, b_=b, d=d2, t=tt: e.copy(
                                d[0][0:64, t * 512:(t + 1) * 512], pst[b_][0:64, :]))
                            sg = tt % 2
                            if stg6_gate[sg] is not None:
                                P.wait("scalar", *stg6_gate[sg])
                            sv = P.tick("scalar", "sS", lambda e, b_=b, g=sg: e.copy(
                                stg6[g][64:128, :], pst[b_][64:128, :]))
                            P.wait("sync", "sS", sv)
                            sem6 = ("x0", "x1")[tt % 2] if part == 0 else ("s0", "s1")[tt % 2]
                            dh = P.dma("sync",
                                       d2[1][0:64, tt * 512:(tt + 1) * 512],
                                       stg6[sg][64:128, :], sem6)
                            stg6_gate[sg] = dh
                            bank_gate[b] = [("sS", sv)]
                            info.setdefault("pdmas", {})[sem6] = dh
                        info["proj"] = sv
                        if part == 2 and tt == 3 and hp + 2 < FC:
                            P.wait("gpsimd", "tS", tv)
                            wa_done[hp + 2] = wa_dma(hp + 2, (hp + 2) % 2)
                    return em

                def passa_chunk(i, qc):
                    def em():
                        ndiag = qc // 4
                        P.wait("tensor", "sS", info["proj"])
                        if i == 1 and qc == 0:
                            for dh_ in info.get("pdmas", {}).values():
                                P.wait("tensor", *dh_)
                        for ktile in range(ndiag + 1):
                            b = nb()
                            if bank_gate[b] is not None:
                                for h_ in bank_gate[b]:
                                    P.wait("tensor", *h_)
                            tn = P.tick("tensor", "tS", lambda e, i_=i, q=qc, k=ktile, b_=b:
                                e.matmul(pst[b_][:, 0:512],
                                         qT6[par][i_][0:64, q * 128:(q + 1) * 128],
                                         kT6[par][i_][0:64, k * 512:(k + 1) * 512],
                                         start=True, stop=True, skip_group_check=True))
                            P.wait("vector", "tS", tn)
                            if ktile == ndiag:
                                P.em("vector", lambda e, b_=b, r=qc - 4 * ktile: e.tensor_tensor(
                                    passA_scr, pst[b_][:, 0:512],
                                    maskneg[:, r * 512:(r + 1) * 512], ALU.add))
                                vv = P.tick("vector", "vS", lambda e, k_=ktile: e.reduce_max(
                                    mq_mat[:, k_:k_ + 1], passA_scr, axis=AX.X))
                            else:
                                vv = P.tick("vector", "vS", lambda e, b_=b, k_=ktile: e.reduce_max(
                                    mq_mat[:, k_:k_ + 1], pst[b_][:, 0:512], axis=AX.X))
                            bank_gate[b] = [("vS", vv)]
                        if ndiag == 0:
                            vcast = P.tick("vector", "vS", lambda e, q8=qc % 8:
                                e.tensor_scalar_mul(mqn8[:, q8:q8 + 1], mq_mat[:, 0:1], -1.0))
                        else:
                            P.em("vector", lambda e: e.tensor_copy(spacer6, maskneg[:, 0:8]))
                            P.em("vector", lambda e, n_=ndiag: e.reduce_max(
                                mq_col[:], mq_mat[:, 0:n_ + 1], axis=AX.X))
                            P.em("vector", lambda e: e.tensor_copy(spacer6, maskneg[:, 0:8]))
                            vcast = P.tick("vector", "vS", lambda e, q8=qc % 8:
                                e.tensor_scalar_mul(mqn8[:, q8:q8 + 1], mq_col[:], -1.0))
                        if qc % 8 == 7:
                            P.wait("tensor", "vS", vcast)
                            P.wait("tensor", "sS", info["mrow"])
                            tmq = 0
                            for q8 in range(8):
                                tr = lambda e, q8_=q8: e.transpose(
                                    pbf[7][64:65, q8_ * 128:q8_ * 128 + 128],
                                    mqn8[:, q8_:q8_ + 1], ident[:, :])
                                if q8 == 7:
                                    tmq = P.tick("tensor", "tS", tr)
                                else:
                                    P.em("tensor", tr)
                            info["tmq"] = tmq
                            P.wait("scalar", "tS", tmq)
                            info["mrow"] = P.tick("scalar", "sS", lambda e, i_=i, q0=qc - 7:
                                e.copy(qT6[par][i_][64:65, q0 * 128:(q0 + 8) * 128],
                                       pbf[7][64:65, 0:1024]))
                            P.wait("vector", "tS", tmq)
                    return em

                def vtok_chunk(i):
                    def em():
                        P.wait("tensor", "sS", info["proj"])
                        P.wait("tensor", "sS", info["mrow"])
                        P.wait("tensor", "vS", info["vt"])
                        idsl = ident[64 * i:64 * i + 64, 64 * i:64 * i + 64]
                        tv = 0
                        for tc in range(16):
                            tr = lambda e, i_=i, t=tc, s=idsl: e.transpose(
                                pbf[7][:, t * 64:t * 64 + 64],
                                vT6[par][64 * i_:64 * i_ + 64, t * 128:(t + 1) * 128], s)
                            if tc == 15:
                                tv = P.tick("tensor", "tS", tr)
                            else:
                                P.em("tensor", tr)
                        P.wait("vector", "tS", tv)
                        vvt = P.tick("vector", "vS", lambda e, i_=i: e.tensor_copy(
                            vt6[par][i_][:].rearrange("p (t c) -> p t c", c=65)[:, :, 0:64],
                            pbf[7][:, 0:1024].rearrange("p (t c) -> p t c", c=64)))
                        P.em("vector", lambda e, i_=i: e.memset(
                            vt6[par][i_][:].rearrange("p (t c) -> p t c", c=65)[:, :, 64:65], 1.0))
                        info["vt"] = vvt
                    return em

                chunks = [proj_chunk(0, tt) for tt in range(4)]
                chunks += [proj_chunk(1, tt) for tt in range(4)]
                chunks += [passa_chunk(i, qc) for i in range(2) for qc in range(16)]
                chunks += [proj_chunk(2, tt) for tt in range(4)]
                chunks += [vtok_chunk(i) for i in range(2)]
                return chunks

            def stage2(hp, next_chunks):
                par = hp % 2
                info = s1_state[par]
                P.wait("tensor", "sS", info["mrow"])
                P.wait("tensor", "vS", info["vt"])
                for i in range(2):
                    for qt in range(4):
                        kmax = 4 * qt + 3
                        pp_free = [0, 0, 0, 0]
                        exp_done = [0] * 16
                        bks = (0, 1, 4 if i == 0 else 3)

                        def emit_out(c, ready):
                            if c == 0 and outbank_gate[0] is not None:
                                for w_ in outbank_gate[0]:
                                    P.wait("tensor", *w_)
                                outbank_gate[0] = None
                            P.wait("tensor", *ready)
                            sl3 = c % 4
                            if i == 0:
                                mo = lambda e, c_=c, s=sl3: e.matmul(
                                    pst[3][0:65, :], vt6[par][0][:, c_ * 65:c_ * 65 + 65],
                                    pp[s][:, :], start=(c_ == 0), stop=(c_ == kmax),
                                    skip_group_check=True)
                                pp_free[sl3] = P.tick("tensor", "tS", mo)
                            else:
                                P.em("tensor", lambda e, c_=c, s=sl3: e.matmul(
                                    pst[4][64:128, :], vt6[par][1][:, c_ * 65:c_ * 65 + 64],
                                    pp[s][:, :], start=(c_ == 0), stop=(c_ == kmax),
                                    skip_group_check=True))
                                pp_free[sl3] = P.tick("tensor", "tS",
                                    lambda e, c_=c, s=sl3: e.matmul(
                                        pst[4][0:1, :], vt6[par][1][:, c_ * 65 + 64:c_ * 65 + 65],
                                        pp[s][:, :], start=(c_ == 0), stop=(c_ == kmax),
                                        skip_group_check=True))

                        pending = []
                        for kc in range(kmax + 1):
                            if next_chunks:
                                next_chunks.pop(0)()
                            bk = bks[kc % 3]
                            if kc == 2 and qt == 0 and outbank_gate[0] is not None:
                                for w_ in outbank_gate[0]:
                                    P.wait("tensor", *w_)
                                outbank_gate[0] = None
                            if kc >= 3:
                                P.wait("tensor", "sS", exp_done[kc - 3])
                            ts_ = P.tick("tensor", "tS", lambda e, i_=i, q=qt, c=kc, b=bk:
                                e.matmul(pst[b][:, 0:512],
                                         kT6[par][i_][0:65, c * 128:(c + 1) * 128],
                                         qT6[par][i_][0:65, q * 512:(q + 1) * 512],
                                         start=True, stop=True, skip_group_check=True))
                            if len(pending) >= 3:
                                emit_out(*pending.pop(0))
                            sl3 = kc % 4
                            P.wait("scalar", "tS", ts_)
                            if pp_free[sl3]:
                                P.wait("scalar", "tS", pp_free[sl3])
                            exp_done[kc] = P.tick("scalar", "sS", lambda e, b=bk, s=sl3:
                                e.activation(pp[s][:, :], pst[b][:, :], AF.Exp,
                                             scale=0.125))
                            ready = ("sS", exp_done[kc])
                            r = kc - 4 * qt
                            if r >= 0:
                                P.wait("vector", "sS", exp_done[kc])
                                vmask = P.tick("vector", "vS", lambda e, s=sl3, r_=r:
                                    e.tensor_tensor(pp[s][:, :], pp[s][:, :],
                                                    mask1[:, r_ * 512:(r_ + 1) * 512],
                                                    ALU.mult))
                                ready = ("vS", vmask)
                            pending.append((kc, ready))
                        if next_chunks:
                            next_chunks.pop(0)()
                        for pe_ in pending:
                            emit_out(*pe_)
                        pending = []
                        t_out = pp_free[kmax % 4]
                        if next_chunks:
                            next_chunks.pop(0)()
                        P.wait("vector", "tS", t_out)
                        if i == 0:
                            P.em("vector", lambda e: e.tensor_single_scalar(
                                dr_row[:], pst[3][64:65, :], 1e-38, ALU.max))
                            v_dr = P.tick("vector", "vS", lambda e: e.reciprocal(
                                dr_row[:], dr_row[:]))
                        else:
                            P.em("vector", lambda e: e.tensor_single_scalar(
                                dr_row[:], pst[4][0:1, :], 1e-38, ALU.max))
                            v_dr = P.tick("vector", "vS", lambda e: e.reciprocal(
                                dr_row[:], dr_row[:]))
                        P.wait("tensor", "vS", v_dr)
                        t_rep = P.tick("tensor", "tS", lambda e, i_=i: e.matmul(
                            pst[5][64 * i_:64 * i_ + 64, :], sel1f[0:1, 0:64],
                            dr_row[:], start=True, stop=True, skip_group_check=True))
                        P.wait("scalar", "tS", t_out)
                        P.wait("scalar", "vS", yv_prev[0])
                        if i == 0:
                            s_num = P.tick("scalar", "sS", lambda e: e.copy(
                                num6[0:64, :], pst[3][0:64, :]))
                        else:
                            s_num = P.tick("scalar", "sS", lambda e: e.copy(
                                num6[64:128, :], pst[4][64:128, :]))
                        P.wait("vector", "sS", s_num)
                        P.wait("vector", "tS", t_rep)
                        yv = P.tick("vector", "vS", lambda e, h=hp, q=qt, i_=i: e.tensor_tensor(
                            yT2(h)[64 * i_:64 * i_ + 64, q * 512:(q + 1) * 512],
                            num6[64 * i_:64 * i_ + 64, :],
                            pst[5][64 * i_:64 * i_ + 64, :], ALU.mult))
                        outbank_gate[0] = [("sS", s_num), ("vS", yv)]
                        yv_prev[0] = yv

            outbank_gate = [None]
            yv_prev = [0]
            cur = make_stage1(0)
            for ch in cur:
                ch()
            for hp in range(FC):
                nxt = make_stage1(hp + 1) if hp + 1 < FC else []
                stage2(hp, nxt)
                for ch in nxt:
                    ch()

            if "y1" in taps:
                tap_sb("y1", A.bf(g_yT2, FC * T))
            tap_sb("q6", qT6[0][0])
            P.barrier()
            A.release(mk6)

            mk7 = A.mark()
            osb7 = [A.f32(A.alloc(1024), 512) for _ in range(4)]
            raw7 = [A.f32(A.alloc(2048), 1024) for _ in range(4)]
            g_sems = ("x0", "x1", "s0", "s1")
            red_prev7 = [0, 0, 0, 0]

            def g_chunk(c):
                def em():
                    sl4 = c % 4
                    P.wait("sync", "vS", red_prev7[sl4])
                    if c < 32:
                        src = dr["w1t"][c // 4][:, (c % 4) * 1024:(c % 4 + 1) * 1024]
                    else:
                        src = dr["w2t"][c - 32]
                    ld = P.dma("sync", raw7[sl4], src, g_sems[sl4])
                    P.wait("vector", ld)
                    red_prev7[sl4] = P.tick("vector", "vS", lambda e, s_=sl4, c_=c:
                        e.reduce_sum(gp2[:, c_:c_ + 1], raw7[s_], axis=AX.X,
                                     apply_absolute_value=True))
                return em
            gch = [g_chunk(c) for c in range(64)]
            osb7_free = [("a0", 0), ("a1", 0), ("a2", 0), ("a3", 0)]
            bank7_free = [None, None, None, None]
            P.wait("tensor", wpj_done)
            for oc in range(FC):
                for tt in range(4):
                    k4 = (oc * 4 + tt) % 4
                    bk = k4
                    if bank7_free[k4] is not None:
                        P.wait("tensor", *bank7_free[k4])
                    tv = 0
                    for fc in range(FC):
                        mm = lambda e, f=fc, o=oc, t=tt, b=bk: e.matmul(
                            pst[b][:, :], wpj_sb[:, f * D + o * 128: f * D + o * 128 + 128],
                            yT2(f)[:, t * 512:(t + 1) * 512],
                            start=(f == 0), stop=(f == FC - 1), skip_group_check=True)
                        if fc == FC - 1:
                            tv = P.tick("tensor", "tS", mm)
                        else:
                            P.em("tensor", mm)
                    P.wait("scalar", "tS", tv)
                    P.wait("scalar", osb7_free[k4])
                    sv = P.tick("scalar", "sS", lambda e, b=bk, k=k4: e.copy(osb7[k], pst[b][:, :]))
                    bank7_free[k4] = ("sS", sv)
                    P.wait("gpsimd", "sS", sv)
                    osb7_free[k4] = P.dma(
                        "gpsimd", xscr[oc * 128:(oc + 1) * 128, tt * 512:(tt + 1) * 512],
                        osb7[k4], "a" + str(k4), accum=True)
                    for _ in range(2):
                        if gch:
                            gch.pop(0)()
            for ch_ in gch:
                ch_()
            A.release(mk7)
            tap_dram("xscr7", xscr)
            tap_dram("xnscr", xnscr)
            P.barrier()

        if stop_phase >= 8:
            A.release(g_xn2)

            def thr_from(gcols, dst, after=None):
                if after is not None:
                    P.wait("tensor", "sS", after)
                tg_ = P.tick("tensor", "tS", lambda e, g=gcols: e.matmul(
                    pst[2][0:1, 0:32], onescol[:], g, start=True, stop=True,
                    skip_group_check=True))
                P.wait("vector", "tS", tg_)
                vg_ = P.tick("vector", "vS", lambda e: e.reduce_sum(
                    t1_row[0:1, 0:1], pst[2][0:1, 0:32], axis=AX.X))
                P.wait("tensor", "vS", vg_)
                tg2 = P.tick("tensor", "tS", lambda e: e.matmul(
                    pst[2][0:128, 0:1], sel1f[:], t1_row[0:1, 0:1],
                    start=True, stop=True, skip_group_check=True))
                P.wait("scalar", "tS", tg2)
                return P.tick("scalar", "sS", lambda e, d=dst: e.activation(
                    d[:], pst[2][:, 0:1], AF.Copy, scale=0.5 / 4194304.0, bias=0.5e-5))

            thr1_done = thr_from(gp2[:, 0:32], thr_sb)
            thr2_done = thr_from(gp2[:, 32:64], thr2_sb, after=thr1_done)
            o_w1q = A.alloc(FC * DFF)
            w1q = A.bf(o_w1q, FC * DFF)
            o_w2q = A.alloc(32 * D)
            w2q = A.bf(o_w2q, 32 * D)
            mkq = A.mark()
            r1 = [A.f32(A.alloc(4096), 2048) for _ in range(2)]
            g1 = [A.bf(A.alloc(2048), 2048) for _ in range(2)]
            r2 = [A.f32(A.alloc(2048), 1024) for _ in range(2)]
            g2 = [A.bf(A.alloc(1024), 1024) for _ in range(2)]
            sq1 = [0, 0]
            sq2 = [0, 0]
            P.wait("vector", "sS", thr2_done)

            def q2_chunk(mat, b):
                sl2 = b % 2
                if mat == 0:
                    raw, sgn, thr = r1[sl2], g1[sl2], thr_sb
                    src = dr["w1t"][b // 2][:, (b % 2) * 2048:(b % 2 + 1) * 2048]
                    dst = w1q[:, b * 2048:(b + 1) * 2048]
                    sems, prev = ("x0", "x1"), sq1
                else:
                    raw, sgn, thr = r2[sl2], g2[sl2], thr2_sb
                    src = dr["w2t"][b]
                    dst = w2q[:, b * 1024:(b + 1) * 1024]
                    sems, prev = ("s0", "s1"), sq2
                P.wait("sync", "vS", prev[sl2])
                ld = P.dma("sync", raw, src, sems[sl2])
                P.wait("scalar", ld)
                P.tick("scalar", "sS", lambda e: e.sign(sgn, raw))
                s2_ = P.tick("scalar", "sS", lambda e: e.activation(raw, raw, AF.Abs))
                P.wait("vector", "sS", s2_)
                P.em("vector", lambda e: e.tensor_single_scalar(
                    dst, raw, thr[:], ALU.is_gt))
                prev[sl2] = P.tick("vector", "vS", lambda e: e.tensor_tensor(
                    dst, dst, sgn, ALU.mult))

            for b_ in range(16):
                q2_chunk(0, b_)
            q1_done = P.cnt["vS"]
            for b_ in range(32):
                q2_chunk(1, b_)
            q2_done = P.cnt["vS"]
            A.release(mkq)
            tap_sb("thr1", thr_sb[:])
            tap_sb("gp1", gp2)
            tap_sb("w1q", A.bf(o_w1q, FC * DFF))
            o_hb8 = A.alloc(32 * 256)
            hb8 = A.bf(o_hb8, 32 * 256)
            o_xnr = [A.alloc(256) for _ in range(2 * FC)]
            xnr = [A.bf(o, 256) for o in o_xnr]
            o_xt8 = [A.alloc(512) for _ in range(2)]
            xt8 = [A.f32(o, 256) for o in o_xt8]
            o_ot8 = [A.alloc(512) for _ in range(2)]
            ot8 = [A.f32(o, 256) for o in o_ot8]
            P.wait("tensor", "vS", q1_done)
            ot_free = [0, 0]
            st_done = [("s0", 0), ("s1", 0)]
            up_done8 = [0, 0]
            for tg in range(8):
                tsl = slice(tg * 256, (tg + 1) * 256)
                tp8 = tg % 2
                P.wait("sync", "tS", up_done8[tp8])
                lds8 = None
                for fc in range(FC):
                    lds8 = P.dma("sync", xnr[tp8 * FC + fc],
                                 xnscr[fc * 128:(fc + 1) * 128, tsl], "x")
                P.wait("tensor", lds8)
                ev_bank = [None, None]
                ev_last = 0
                for hc in range(32):
                    bk = hc % 2
                    if ev_bank[bk] is not None:
                        P.wait("tensor", *ev_bank[bk])
                    tv = 0
                    for fc in range(FC):
                        mm = lambda e, f=fc, h=hc, b=bk, p8=tp8: e.matmul(
                            pst[b][:, 0:256], w1q[:, f * DFF + h * 128: f * DFF + h * 128 + 128],
                            xnr[p8 * FC + f], start=(f == 0), stop=(f == FC - 1),
                            skip_group_check=True)
                        if fc == FC - 1:
                            tv = P.tick("tensor", "tS", mm)
                        else:
                            P.em("tensor", mm)
                    P.wait("scalar", "tS", tv)
                    ev_last = P.tick("scalar", "sS", lambda e, h=hc, b=bk: e.activation(
                        hb8[:, h * 256:(h + 1) * 256], pst[b][:, 0:256], AF.Relu,
                        bias=b_b1[:, h:h + 1]))
                    ev_bank[bk] = ("sS", ev_last)
                up_done8[tp8] = tv
                P.wait("tensor", "sS", ev_last)
                if tg == 0:
                    P.wait("tensor", "vS", q2_done)
                for dch in range(2):
                    tdn = 0
                    for hc in range(32):
                        for dc4 in range(4):
                            dc = dch * 4 + dc4
                            mm = lambda e, h=hc, d=dc, b=2 + dc4: e.matmul(
                                pst[b][:, 0:256], w2q[:, h * D + d * 128: h * D + d * 128 + 128],
                                hb8[:, h * 256:(h + 1) * 256],
                                start=(h == 0), stop=(h == 31), skip_group_check=True)
                            if hc == 31 and dc4 == 3:
                                tdn = P.tick("tensor", "tS", mm)
                            else:
                                P.em("tensor", mm)
                    for dc4 in range(4):
                        dc = dch * 4 + dc4
                        sx = dc4 % 2
                        P.wait("sync", "vS", ot_free[sx])
                        xld = P.dma("sync", xt8[sx], xscr[dc * 128:(dc + 1) * 128, tsl],
                                    "x" + str(sx))
                        P.wait("vector", xld)
                        P.wait("vector", st_done[sx])
                        P.wait("vector", "tS", tdn)
                        vv = P.tick("vector", "vS", lambda e, b=2 + dc4, d=dc, s=sx:
                            e.scalar_tensor_tensor(ot8[s], pst[b][:, 0:256],
                                                   b_b2[:, d:d + 1], xt8[s],
                                                   ALU.add, ALU.add))
                        ot_free[sx] = vv
                        P.wait("sync", "vS", vv)
                        st_done[sx] = P.dma("sync", out[dc * 128:(dc + 1) * 128, tsl],
                                            ot8[sx], "s" + str(sx))
                        P.wait("tensor", "vS", vv)

        for name, (dram_ap, src_ap, kind) in dbg.items():
            if kind == "sb":
                P.dma("sync", dram_ap, src_ap, "st")
        mkb = A.mark()
        bounce_f = A.f32(A.alloc(4096), 2048)
        bounce_b = A.bf(A.alloc(2048), 2048)
        for name, (dram_ap, src_ap, kind) in dbg.items():
            if kind == "dram":
                bounce = bounce_f if src_ap.dtype == F32 else bounce_b
                for fc in range(src_ap.shape[0] // 128):
                    lv = P.dma("sync", bounce, src_ap[fc * 128:(fc + 1) * 128, :], "x")
                    P.wait("sync", lv)
                    sv_ = P.dma("sync", dram_ap[fc * 128:(fc + 1) * 128, :], bounce, "st")
                    P.wait("sync", sv_)
        A.release(mkb)
        for s_ in ("c", "x", "x0", "x1", "st", "s0", "s1"):
            P.wait("sync", s_, P.cnt[s_])
        for s_ in ("w0", "w1", "w2", "w3", "a0", "a1", "a2", "a3"):
            P.wait("gpsimd", s_, P.cnt[s_])

        with nc.Block() as block:
            for eng in ENGINES:
                def runner(e, eng=eng):
                    for f in P.streams[eng]:
                        f(e)
                getattr(block, eng)(runner)
    return nc


def host_inputs(inp, core):
    x = inp["L0_attn_x"]
    d = {
        "xin": np.ascontiguousarray(np.asarray(x[core]).T),
        "wq": _blk(np.asarray(inp["L0_attn_W_q"]).T),
        "wk": _blk(np.asarray(inp["L0_attn_W_k"]).T),
        "wv": _blk(np.asarray(inp["L0_attn_W_v"]).T),
        "wo": _blk(np.asarray(inp["L0_attn_W_o"]).T),
        "rwt": _blk(np.asarray(inp["L0_ffn_router_w"])),
        "rbias": np.ascontiguousarray(np.asarray(inp["L0_ffn_router_b"]).reshape(1, E)),
        "sw1": _blk(np.asarray(inp["L0_ffn_sw1"])),
        "sw2": _blk(np.asarray(inp["L0_ffn_sw2"])),
        "rw1": np.ascontiguousarray(np.asarray(inp["L0_ffn_rw1"]).reshape(E, FC, 128, HM)),
        "rw2": np.ascontiguousarray(np.asarray(inp["L0_ffn_rw2"]).reshape(E, HM // 128, 128, D)),
        "rb1": np.ascontiguousarray(np.asarray(inp["L0_ffn_rb1"]).reshape(E, 16, 128)
                                    .transpose(2, 0, 1).reshape(128, E * 16)),
        "rb2": np.ascontiguousarray(np.asarray(inp["L0_ffn_rb2"])),
        "sb1": np.ascontiguousarray(np.asarray(inp["L0_ffn_sb1"]).reshape(16, 128).T),
        "sb2": np.ascontiguousarray(np.asarray(inp["L0_ffn_sb2"]).reshape(1, D)),
        "w1t": _blk(np.asarray(inp["L1_ffn_W1"]).T),
        "w2t": _blk(np.asarray(inp["L1_ffn_W2"]).T),
        "b1": np.ascontiguousarray(np.asarray(inp["L1_ffn_b1"]).reshape(32, 128).T),
        "b2": np.ascontiguousarray(np.asarray(inp["L1_ffn_b2"]).reshape(8, 128).T),
        "wproj": _blk(np.asarray(inp["L1_attn_w_proj"]).T),
    }
    wat = np.asarray(inp["L1_attn_w_attn"]).T
    wa = np.empty((FC, FC, 128, 384), np.float32)
    for hp in range(FC):
        cols = np.concatenate([
            wat[:, 128 * hp:128 * hp + 128],
            wat[:, 1024 + 128 * hp:1024 + 128 * hp + 128],
            wat[:, 2048 + 128 * hp:2048 + 128 * hp + 128]], axis=1)
        wa[hp] = cols.reshape(FC, 128, 384)
    d["wa"] = np.ascontiguousarray(wa)
    return d


def kernel(**inputs):
    nc = build()
    in_maps = [host_inputs(inputs, c) for c in range(NCORES)]
    res = run_bass_kernel_spmd(nc, in_maps, core_ids=list(range(NCORES)))
    return np.stack([np.asarray(res.results[c]["out"]).T for c in range(NCORES)])


# version 17 (final)
# speedup vs baseline: 1.0601x; 1.0601x over previous
import contextlib
import numpy as np
import ml_dtypes
import concourse.bass as bass
import concourse.mybir as mybir
from concourse.bass_utils import run_bass_kernel_spmd

F32 = mybir.dt.float32
BF16 = mybir.dt.bfloat16
AF = mybir.ActivationFunctionType
ALU = mybir.AluOpType
AX = mybir.AxisListType
bfdt = ml_dtypes.bfloat16

D = 1024
T = 2048
FC = 8
H = 16
DK = 64
CH = 64
NCH = T // CH
E = 8
HM = 2048
DFF = 4096
NCORES = 8

ENGINES = ("sync", "gpsimd", "tensor", "vector", "scalar")


def _blk(w):
    r, c = w.shape
    return np.ascontiguousarray(w.reshape(r // 128, 128, c))


class Arena:

    def __init__(self, tensor, ncols):
        self.t = tensor
        self.ncols = ncols
        self.top = 0
        self.peak = 0

    def alloc(self, cols):
        cols = (cols + 15) & ~15
        off = self.top
        self.top += cols
        self.peak = max(self.peak, self.top)
        assert self.top <= self.ncols, f"arena overflow: {self.top} > {self.ncols}"
        return off

    def bf(self, off, cols, parts=128):
        return self.t[0:parts, off:off + cols]

    def f32(self, off, cols, parts=128):
        return self.t[0:parts, off:off + 2 * cols].bitcast(F32)

    def mark(self):
        return self.top

    def release(self, mark):
        self.top = mark


DMA_SEMS = ("c", "x", "x0", "x1", "st", "s0", "s1",
            "w0", "w1", "w2", "w3", "a0", "a1", "a2", "a3")


class Plan:
    def __init__(self, nc, ctx):
        self.nc = nc
        self.streams = {e: [] for e in ENGINES}
        self.sem = {}
        self.cnt = {}
        for name in DMA_SEMS + ("tS", "vS", "sS", "bar"):
            self.sem[name] = ctx.enter_context(nc.semaphore(name))
            self.cnt[name] = 0
        self._wrot = 0

    def next_wsem(self):
        s = "w" + str(self._wrot % 4)
        self._wrot += 1
        return s

    def em(self, eng, fn):
        self.streams[eng].append(fn)

    def dma(self, queue, out_ap, in_ap, sem, accum=False):
        self.cnt[sem] += 16
        val = self.cnt[sem]
        s = self.sem[sem]
        if accum:
            self.em(queue, lambda e, o=out_ap, i=in_ap, s_=s: e.dma_start(
                out=o, in_=i, accum_op=ALU.add).then_inc(s_, 16))
        else:
            self.em(queue, lambda e, o=out_ap, i=in_ap, s_=s: e.dma_start(
                out=o, in_=i).then_inc(s_, 16))
        return (sem, val)

    def wait(self, eng, key, val=None):
        if val is None:
            key, val = key
        if val > 0:
            self.em(eng, lambda e, s=self.sem[key], v=val: e.wait_ge(s, v))

    def tick(self, eng, key, instr_fn):
        self.cnt[key] += 1
        val = self.cnt[key]
        self.em(eng, lambda e, f=instr_fn, s=self.sem[key]: f(e).then_inc(s, 1))
        return val

    def inc(self, eng, key):
        self.cnt[key] += 1
        val = self.cnt[key]
        if eng == "tensor":
            self.em(eng, lambda e: e.drain())
        self.em(eng, lambda e, s=self.sem[key], v=val: e.sem_inc(s, 1))
        return val

    def barrier(self):
        for s in ("c", "x", "x0", "x1", "st", "s0", "s1"):
            self.wait("sync", s, self.cnt[s])
        for s in ("w0", "w1", "w2", "w3", "a0", "a1", "a2", "a3"):
            self.wait("gpsimd", s, self.cnt[s])
        self.cnt["bar"] += 5
        tgt = self.cnt["bar"]
        self.em("tensor", lambda e: e.drain())
        for eng in ENGINES:
            self.em(eng, lambda e, s=self.sem["bar"]: e.sem_inc(s, 1))
        for eng in ENGINES:
            self.em(eng, lambda e, s=self.sem["bar"], v=tgt: e.wait_ge(s, v))


def build(stop_phase=99, taps=(), l0_stage=3, l0_pairs=FC, l0_kinds=3, l0_ops=7):
    nc = bass.Bass("TRN2", target_bir_lowering=False, detect_race_conditions=False)
    taps = set(taps)
    dbg = {}

    dr = {}
    dr["xin"] = nc.dram_tensor("xin", [D, T], F32, kind="ExternalInput").ap()
    for nm, shp in [
        ("wq", [FC, 128, D]), ("wk", [FC, 128, D]), ("wv", [FC, 128, D]),
        ("wo", [FC, 128, D]),
        ("rwt", [FC, 128, E]), ("rbias", [1, E]),
        ("sw1", [FC, 128, HM]), ("sw2", [HM // 128, 128, D]),
        ("rw1", [E, FC, 128, HM]), ("rw2", [E, HM // 128, 128, D]),
        ("rb1", [128, E * 16]), ("rb2", [8, D]),
        ("sb1", [128, 16]), ("sb2", [1, D]),
        ("wa", [FC, FC, 128, 384]), ("wproj", [FC, 128, D]),
        ("w1t", [FC, 128, DFF]), ("w2t", [DFF // 128, 128, D]),
        ("b1", [128, 32]), ("b2", [128, 8]),
    ]:
        dr[nm] = nc.dram_tensor(nm, shp, F32, kind="ExternalInput").ap()
    out = nc.dram_tensor("out", [D, T], F32, kind="ExternalOutput").ap()
    xscr = nc.dram_tensor("xscr", [D, T], F32, kind="Internal").ap()
    xnscr = nc.dram_tensor("xnscr", [D, T], BF16, kind="Internal").ap()
    x1snap = nc.dram_tensor("x1snap", [D, T], F32, kind="Internal").ap()
    x2snap = nc.dram_tensor("x2snap", [D, T], F32, kind="Internal").ap()

    c_ident = nc.inline_tensor(np.eye(128, dtype=bfdt), "c_ident").ap()
    c_onescol = nc.inline_tensor(np.ones((128, 1), np.float32), "c_onescol").ap()
    c_sel1f = nc.inline_tensor(np.ones((1, 128), np.float32), "c_sel1f").ap()
    c_sel1b = nc.inline_tensor(np.ones((1, 128), bfdt), "c_sel1b").ap()
    c_onesrow = nc.inline_tensor(np.ones((1, 512), bfdt), "c_onesrow").ap()
    m0 = np.triu(np.ones((128, 128), np.float32))
    m0d = np.concatenate([m0, m0], axis=1).astype(bfdt)
    c_mask0 = nc.inline_tensor(m0d, "c_mask0").ap()
    ii = np.arange(128)[:, None]
    jj = np.arange(512)[None, :]
    m1 = np.stack([(128 * r + ii <= jj) for r in range(4)]).astype(bfdt)
    c_mask1 = nc.inline_tensor(np.ascontiguousarray(m1), "c_mask1").ap()
    mneg = np.ascontiguousarray(((m1.astype(np.float32) - 1.0) * 30000.0))
    c_maskneg = nc.inline_tensor(mneg, "c_maskneg").ap()
    c_onesall = nc.inline_tensor(np.ones((128, 128), bfdt), "c_onesall").ap()
    s8 = np.zeros((E, E, 128), np.float32)
    for e_ in range(E):
        s8[e_, e_, :] = 1.0
    c_sel8 = nc.inline_tensor(s8.astype(bfdt), "c_sel8").ap()

    with contextlib.ExitStack() as ctx:
        P = Plan(nc, ctx)
        sbt = lambda name, shape, dt: ctx.enter_context(nc.sbuf_tensor(name, shape, dt))

        ident = sbt("ident", [128, 128], BF16)
        onescol = sbt("onescol", [128, 1], F32)
        sel1f = sbt("sel1f", [1, 128], F32)
        sel1b = sbt("sel1b", [1, 128], BF16)
        onesrow = sbt("onesrow", [1, 512], BF16)
        mask0 = sbt("mask0", [128, 256], BF16)
        onesall = sbt("onesall", [128, 128], BF16)
        sel8 = sbt("sel8", [8, E * 128], BF16)
        b_rb1 = sbt("b_rb1", [128, E * 16], F32)
        b_sb1 = sbt("b_sb1", [128, 16], F32)
        b_b1 = sbt("b_b1", [128, 32], F32)
        b_b2 = sbt("b_b2", [128, 8], F32)
        rb2_sb = sbt("rb2_sb", [8, D], BF16)
        sb2_sb = sbt("sb2_sb", [1, D], BF16)
        rbias_sb = sbt("rbias_sb", [1, E], BF16)
        wT_sb = sbt("wT_sb", [8, T], BF16)
        m_row = sbt("m_row", [1, 512], F32)
        v_row = sbt("v_row", [1, 512], F32)
        r_row = sbt("r_row", [1, 512], F32)
        s_row = sbt("s_row", [1, 512], F32)
        rb_row = sbt("rb_row", [1, 512], BF16)
        sb_row = sbt("sb_row", [1, 512], BF16)
        t1_row = sbt("t1_row", [1, 512], F32)
        dr_row = sbt("dr_row", [1, 512], F32)
        drB_row = sbt("drB_row", [1, 512], F32)
        nrm_acc = sbt("nrm_acc", [1, 8], F32)
        negm_sb = sbt("negm_sb", [128, 1], F32)
        thr_sb = sbt("thr_sb", [128, 1], F32)
        thr2_sb = sbt("thr2_sb", [128, 1], F32)
        gp2 = sbt("gp2", [128, 64], F32)
        g_row = sbt("g_row", [1, 8], F32)

        pst = [ctx.enter_context(nc.psum_tensor(f"pst{i}", [128, 512], F32))
               for i in range(8)]
        pbf = [p[:, 0:512].bitcast(BF16) for p in pst]

        ARENA_COLS = 88576
        arena_t = sbt("arena", [128, ARENA_COLS], BF16)
        A = Arena(arena_t, ARENA_COLS)

        def tap_sb(name, ap):
            if name in taps:
                dbg[name] = (nc.dram_tensor("dbg_" + name, list(ap.shape), ap.dtype,
                                            kind="ExternalOutput").ap(), ap, "sb")

        def tap_dram(name, dram_ap):
            if name in taps:
                dbg[name] = (nc.dram_tensor("dbg_" + name, list(dram_ap.shape),
                                            dram_ap.dtype, kind="ExternalOutput").ap(),
                             dram_ap, "dram")

        for dst, src in ((ident, c_ident), (onescol, c_onescol), (sel1f, c_sel1f),
                         (sel1b, c_sel1b), (onesrow, c_onesrow), (mask0, c_mask0)):
            P.dma("sync", dst[:], src, "c")
        P.dma("sync", onesall[:], c_onesall, "c")
        P.dma("sync", sel8[:].rearrange("p (e c) -> p e c", e=E),
              c_sel8.rearrange("e p c -> p e c"), "c")
        P.dma("sync", b_rb1[:], dr["rb1"], "c")
        P.dma("sync", b_sb1[:], dr["sb1"], "c")
        P.dma("sync", b_b1[:], dr["b1"], "c")
        cst_s = P.dma("sync", b_b2[:], dr["b2"], "c")
        for eng in ("tensor", "vector", "scalar", "gpsimd"):
            P.wait(eng, cst_s)

        def blk_cast(sb_ap_fn, dram_ap, nblk, cols, sem):
            v = None
            for b in range(nblk):
                v = P.dma("gpsimd", sb_ap_fn(b), dram_ap[b], sem)
            return v

        def emit_layernorm(src_dram, xn_view, store_xscr=False, store_xn=False,
                           snap_dram=None):
            mk = A.mark()
            xt_off = [A.alloc(1024) for _ in range(FC)]
            x2_off = [A.alloc(1024) for _ in range(2)]
            tmp_off = A.alloc(512)
            xt = [A.f32(o, 512) for o in xt_off]
            x2 = [A.f32(o, 512) for o in x2_off]
            tmp = A.bf(tmp_off, 512)
            if "lnstat" in taps and "lnstat" not in dbg:
                st_off = A.alloc(8192)
                lnstat = A.f32(st_off, 4096, parts=1)
                tap_sb("lnstat", lnstat)
            else:
                lnstat = None
            for tt in range(4):
                sl = slice(tt * 512, (tt + 1) * 512)
                ldgrp = None
                for fc in range(FC):
                    ldgrp = P.dma("sync", xt[fc], src_dram[fc * 128:(fc + 1) * 128, sl], "x")
                P.wait("tensor", ldgrp)
                P.wait("scalar", ldgrp)
                tstat = 0
                mm2_done = [0] * FC
                for fc in range(FC):
                    P.em("tensor", lambda e, f=fc: e.matmul(
                        pst[2][0:1, :], onescol[:], xt[f],
                        start=(f == 0), stop=(f == FC - 1), skip_group_check=True))
                    if fc >= 2:
                        P.wait("scalar", "tS", mm2_done[fc - 2])
                    sv = P.tick("scalar", "sS", lambda e, f=fc: e.activation(
                        x2[f % 2], xt[f], AF.Square))
                    P.wait("tensor", "sS", sv)
                    mm2 = lambda e, f=fc: e.matmul(
                        pst[3][0:1, :], onescol[:], x2[f % 2],
                        start=(f == 0), stop=(f == FC - 1), skip_group_check=True)
                    mm2_done[fc] = P.tick("tensor", "tS", mm2)
                tstat = mm2_done[FC - 1]
                P.wait("vector", "tS", tstat)
                P.em("vector", lambda e: e.tensor_scalar_mul(m_row[:], pst[2][0:1, :], 1.0 / D))
                P.em("vector", lambda e: e.tensor_scalar_mul(v_row[:], pst[3][0:1, :], 1.0 / D))
                P.em("vector", lambda e: e.tensor_tensor(t1_row[:], m_row[:], m_row[:], ALU.mult))
                P.em("vector", lambda e: e.tensor_tensor(v_row[:], v_row[:], t1_row[:], ALU.subtract))
                vv = P.tick("vector", "vS", lambda e: e.tensor_scalar_add(v_row[:], v_row[:], 1e-5))
                P.wait("scalar", "vS", vv)
                sv = P.tick("scalar", "sS", lambda e: e.activation(v_row[:], v_row[:], AF.Sqrt))
                P.wait("vector", "sS", sv)
                P.em("vector", lambda e: e.reciprocal(r_row[:], v_row[:]))
                if lnstat is not None:
                    P.em("vector", lambda e, t=tt: e.tensor_copy(
                        lnstat[0:1, t * 512:(t + 1) * 512], m_row[:]))
                    P.em("vector", lambda e, t=tt: e.tensor_copy(
                        lnstat[0:1, 2048 + t * 512:2048 + (t + 1) * 512], r_row[:]))
                P.em("vector", lambda e: e.tensor_tensor(s_row[:], m_row[:], r_row[:], ALU.mult))
                P.em("vector", lambda e: e.tensor_scalar_mul(s_row[:], s_row[:], -1.0))
                P.em("vector", lambda e: e.tensor_copy(rb_row[:], r_row[:]))
                vv = P.tick("vector", "vS", lambda e: e.tensor_copy(sb_row[:], s_row[:]))
                P.wait("tensor", "vS", vv)
                P.em("tensor", lambda e: e.matmul(pst[4][:, :], sel1b[:], rb_row[:],
                                                  start=True, stop=True, skip_group_check=True))
                trep = P.tick("tensor", "tS", lambda e: e.matmul(
                    pst[5][:, :], sel1b[:], sb_row[:], start=True, stop=True,
                    skip_group_check=True))
                P.wait("vector", "tS", trep)
                last_v = 0
                for fc in range(FC):
                    dst = xn_view(fc)[:, sl]
                    P.em("vector", lambda e, f=fc: e.tensor_tensor(
                        tmp, xt[f], pst[4][:, :], ALU.mult))
                    last_v = P.tick("vector", "vS", lambda e, d=dst, f=fc: e.tensor_tensor(
                        d, tmp, pst[5][:, :], ALU.add))
                    if store_xn:
                        P.wait("sync", "vS", last_v)
                        P.dma("sync", xnscr[fc * 128:(fc + 1) * 128, sl], dst, "st")
                    if store_xscr:
                        P.wait("sync", "vS", last_v)
                        P.dma("sync", xscr[fc * 128:(fc + 1) * 128, sl], xt[fc], "st")
                    if snap_dram is not None:
                        P.wait("sync", "vS", last_v)
                        P.dma("sync", snap_dram[fc * 128:(fc + 1) * 128, sl], xt[fc], "st")
                P.wait("tensor", "vS", last_v)
                P.wait("sync", "vS", last_v)
                P.wait("sync", "st", P.cnt["st"])
            A.release(mk)

        g_xn = A.alloc(FC * T)
        mkL0 = A.mark()
        g_wq = A.alloc(FC * D); g_wk = A.alloc(FC * D)
        g_wv = A.alloc(FC * D); g_wo = A.alloc(FC * D)
        g_rwt = A.alloc(FC * E)
        g_yT = A.alloc(FC * T)
        wq_sb = A.bf(g_wq, FC * D); wk_sb = A.bf(g_wk, FC * D)
        wv_sb = A.bf(g_wv, FC * D); wo_sb = A.bf(g_wo, FC * D)
        rwt_sb = A.bf(g_rwt, FC * E)
        xn = lambda fc: A.bf(g_xn + fc * T, T)
        yT = lambda fc: A.bf(g_yT + fc * T, T)

        wsem0 = P.next_wsem()
        P.dma("gpsimd", rb2_sb[:], dr["rb2"], wsem0)
        P.dma("gpsimd", sb2_sb[:], dr["sb2"], wsem0)
        P.dma("gpsimd", rbias_sb[:], dr["rbias"], wsem0)
        blk_cast(lambda b: wq_sb[:, b * D:(b + 1) * D], dr["wq"], FC, D, wsem0)
        blk_cast(lambda b: wk_sb[:, b * D:(b + 1) * D], dr["wk"], FC, D, wsem0)
        blk_cast(lambda b: wv_sb[:, b * D:(b + 1) * D], dr["wv"], FC, D, wsem0)
        blk_cast(lambda b: wo_sb[:, b * D:(b + 1) * D], dr["wo"], FC, D, wsem0)
        w0_done = blk_cast(lambda b: rwt_sb[:, b * E:(b + 1) * E], dr["rwt"], FC, E, wsem0)
        wq_done = wk_done = wv_done = wo_done = rwt_done = w0_done

        emit_layernorm(dr["xin"], xn, store_xscr=True)
        if "xn0" in taps:
            tap_sb("xn0", A.bf(g_xn, FC * T))
        P.barrier()

        if stop_phase >= 2:
            mk2 = A.mark()
            o_qf = A.alloc(T); o_kf = A.alloc(T); o_vf = A.alloc(T)
            o_kt = [A.alloc(1024) for _ in range(2)]
            o_vt = [A.alloc(1040) for _ in range(2)]
            o_st = A.alloc(80)
            o_at2 = [A.alloc(256), A.alloc(256)]
            o_num = A.alloc(1024)
            o_el1 = A.alloc(2048)
            o_el2 = A.alloc(1024)
            qf = A.bf(o_qf, T); kf = A.bf(o_kf, T); vf = A.bf(o_vf, T)
            kt = [A.bf(o, 1024) for o in o_kt]
            vt = [A.bf(o, 1040) for o in o_vt]
            stil = A.bf(o_st, 80)
            atil2 = [A.bf(o, 256) for o in o_at2]
            num_sb = A.f32(o_num, 512)
            el1p = [A.f32(o_el1, 512), A.f32(o_el1 + 1024, 512)]
            el2p = [A.bf(o_el2, 512), A.bf(o_el2 + 512, 512)]

            P.wait("tensor", wv_done)
            bank_free = {0: None, 1: None}
            for hp in range(l0_pairs):
                for kind in range(l0_kinds):
                    wsb = (wq_sb, wk_sb, wv_sb)[kind]
                    dst = (qf, kf, vf)[kind]
                    for tt in range(4):
                        bk = (kind * 4 + tt) % 2
                        if bank_free[bk] is not None:
                            P.wait("tensor", *bank_free[bk])
                        tv = 0
                        for fc in range(FC):
                            mm = lambda e, f=fc, b=bk, w=wsb, h=hp, t=tt: e.matmul(
                                pst[b][:, :], w[:, f * D + h * 128: f * D + h * 128 + 128],
                                xn(f)[:, t * 512:(t + 1) * 512],
                                start=(f == 0), stop=(f == FC - 1), skip_group_check=True)
                            if fc == FC - 1:
                                tv = P.tick("tensor", "tS", mm)
                            else:
                                P.em("tensor", mm)
                        sl = slice(tt * 512, (tt + 1) * 512)
                        if kind < 2:
                            ep = tt % 2
                            P.wait("vector", "tS", tv)
                            vv = P.tick("vector", "vS", lambda e, b=bk, p_=ep: e.tensor_single_scalar(
                                el1p[p_], pst[b][:, :], 0.0, ALU.min))
                            P.wait("scalar", "vS", vv)
                            sv = P.tick("scalar", "sS", lambda e, p_=ep: e.activation(
                                el2p[p_], el1p[p_], AF.Exp))
                            P.wait("vector", "sS", sv)
                            bank_free[bk] = ("vS", P.tick("vector", "vS",
                                lambda e, b=bk, d=dst, s=sl, p_=ep:
                                e.scalar_tensor_tensor(d[:, s], pst[b][:, :], 0.0, el2p[p_],
                                                       ALU.max, ALU.add)))
                        else:
                            P.wait("vector", "tS", tv)
                            sv = P.tick("vector", "vS", lambda e, b=bk, d=dst, s=sl:
                                e.tensor_copy(d[:, s], pst[b][:, :]))
                            bank_free[bk] = ("vS", sv)
                vready = P.inc("vector", "vS")
                sready = P.inc("scalar", "sS")
                P.wait("tensor", "vS", vready)
                P.wait("tensor", "sS", sready)
                if l0_stage < 2:
                    continue
                for i in range(2):
                    idsl = ident[64 * i:64 * i + 64, 64 * i:64 * i + 64]
                    for tc in range(16):
                        tr = lambda e, i_=i, t=tc, s=idsl: e.transpose(
                            pbf[6][:, t * 64:t * 64 + 64],
                            kf[64 * i_:64 * i_ + 64, t * 128:(t + 1) * 128], s)
                        if tc == 15:
                            tv = P.tick("tensor", "tS", tr)
                        else:
                            P.em("tensor", tr)
                    P.wait("scalar", "tS", tv)
                    skt = P.tick("scalar", "sS", lambda e, i_=i: e.copy(kt[i_][:, :], pbf[6][:, 0:1024]))
                    for tc in range(16):
                        tr = lambda e, i_=i, t=tc, s=idsl: e.transpose(
                            pbf[7][:, t * 64:t * 64 + 64],
                            vf[64 * i_:64 * i_ + 64, t * 128:(t + 1) * 128], s)
                        if tc == 15:
                            tv = P.tick("tensor", "tS", tr)
                        else:
                            P.em("tensor", tr)
                    P.wait("vector", "tS", tv)
                    vvt = P.tick("vector", "vS", lambda e, i_=i: e.tensor_copy(
                        vt[i_][:].rearrange("p (t c) -> p t c", c=65)[:, :, 0:64],
                        pbf[7][:, 0:1024].rearrange("p (t c) -> p t c", c=64)))
                    P.em("vector", lambda e, i_=i: e.memset(
                        vt[i_][:].rearrange("p (t c) -> p t c", c=65)[:, :, 64:65], 1.0))
                    P.wait("tensor", "sS", skt)
                    P.wait("tensor", "vS", vvt)
                if l0_stage < 3:
                    continue
                vz = P.tick("vector", "vS", lambda e: e.memset(stil[:, :], 0.0))
                P.wait("tensor", "vS", vz)
                s2_last = 0
                vmask_done = [0, 0]
                pend_intra = None

                def emit_intra(n_, vm, last):
                    ap_ = n_ % 2
                    cc_ = slice(128 * (n_ % 4), 128 * (n_ % 4) + 128)
                    P.wait("tensor", "vS", vm)
                    P.em("tensor", lambda e, c=cc_, a=atil2[ap_]: e.matmul(
                        pst[3][0:65, c], vt[0][:, n_ * 65:n_ * 65 + 65],
                        a[:, 0:128], start=False, stop=True, skip_group_check=True))
                    P.em("tensor", lambda e, c=cc_, a=atil2[ap_]: e.matmul(
                        pst[4][64:128, c], vt[1][:, n_ * 65:n_ * 65 + 64],
                        a[:, 128:256], start=False, stop=True, skip_group_check=True))
                    mmz = lambda e, c=cc_, a=atil2[ap_]: e.matmul(
                        pst[4][0:1, c], vt[1][:, n_ * 65 + 64:n_ * 65 + 65],
                        a[:, 128:256], start=False, stop=True, skip_group_check=True)
                    if last:
                        return P.tick("tensor", "tS", mmz)
                    P.em("tensor", mmz)
                    return 0

                for cg in range(4):
                    for n4 in range(4):
                        n = cg * 4 + n4
                        abank = pst[n % 2]
                        nsl = slice(n * 128, (n + 1) * 128)
                        bbank = pst[6 + n % 2]
                        P.wait("tensor", "vS", vmask_done[n % 2])
                        for i in range(2):
                            mm = lambda e, i_=i, s=nsl, b=(abank if i == 0 else bbank): \
                                e.matmul(b[:, 0:128],
                                         kf[64 * i_:64 * i_ + 64, s],
                                         qf[64 * i_:64 * i_ + 64, s],
                                         start=True, stop=True, skip_group_check=True)
                            if i == 1:
                                t_attn = P.tick("tensor", "tS", mm)
                            else:
                                P.em("tensor", mm)
                        cc = slice(128 * n4, 128 * n4 + 128)
                        P.wait("tensor", "sS", s2_last)
                        P.em("tensor", lambda e, s=nsl, c=cc, st=(n4 == 0): e.matmul(
                            pst[3][0:65, c], stil[0:64, 0:65], qf[0:64, s],
                            start=st, stop=False, skip_group_check=True))
                        P.em("tensor", lambda e, s=nsl, c=cc, st=(n4 == 0): e.matmul(
                            pst[4][64:128, c], stil[64:128, 0:64], qf[64:128, s],
                            start=st, stop=False, skip_group_check=True))
                        P.em("tensor", lambda e, s=nsl, c=cc, st=(n4 == 0): e.matmul(
                            pst[4][0:1, c], stil[64:128, 64:65], qf[64:128, s],
                            start=st, stop=False, skip_group_check=True))
                        for i in range(2):
                            mm = lambda e, i_=i, n_=n: e.matmul(
                                pst[2][64 * i_:64 * i_ + 64, 0:65],
                                kt[i_][:, n_ * 64:(n_ + 1) * 64],
                                vt[i_][:, n_ * 65:n_ * 65 + 65],
                                start=(n_ == 0), stop=(n_ == 15),
                                skip_group_check=True)
                            if i == 1:
                                t_kv = P.tick("tensor", "tS", mm)
                            else:
                                P.em("tensor", mm)
                        if pend_intra is not None:
                            emit_intra(pend_intra[0], pend_intra[1], False)
                            pend_intra = None
                        P.wait("vector", "tS", t_attn)
                        P.em("vector", lambda e, b=abank, a=atil2[n % 2]: e.tensor_tensor(
                            a[:, 0:128], b[:, 0:128], mask0[:, 0:128], ALU.mult))
                        vmask_done[n % 2] = P.tick("vector", "vS",
                            lambda e, b=bbank, a=atil2[n % 2]: e.tensor_tensor(
                                a[:, 128:256], b[:, 0:128], mask0[:, 128:256], ALU.mult))
                        P.wait("scalar", "tS", t_kv)
                        s2_last = P.tick("scalar", "sS", lambda e: e.copy(
                            stil[:, 0:65], pst[2][:, 0:65]))
                        pend_intra = (n, vmask_done[n % 2])
                    t_cg = emit_intra(pend_intra[0], pend_intra[1], True)
                    pend_intra = None
                    if False:
                        pass
                    if not (l0_ops & 4):
                        continue
                    P.wait("vector", "tS", t_cg)
                    P.em("vector", lambda e: e.tensor_single_scalar(
                        dr_row[:], pst[3][64:65, :], 1e-6, ALU.max))
                    P.em("vector", lambda e: e.reciprocal(dr_row[:], dr_row[:]))
                    P.em("vector", lambda e: e.tensor_single_scalar(
                        drB_row[:], pst[4][0:1, :], 1e-6, ALU.max))
                    v_dr = P.tick("vector", "vS", lambda e: e.reciprocal(drB_row[:], drB_row[:]))
                    P.wait("tensor", "vS", v_dr)
                    P.em("tensor", lambda e: e.matmul(pst[5][0:64, :], sel1f[0:1, 0:64],
                                                      dr_row[:], start=True, stop=True,
                                                      skip_group_check=True))
                    t_rep = P.tick("tensor", "tS", lambda e: e.matmul(
                        pst[5][64:128, :], sel1f[0:1, 0:64], drB_row[:],
                        start=True, stop=True, skip_group_check=True))
                    P.wait("scalar", "tS", t_cg)
                    P.em("scalar", lambda e: e.copy(num_sb[0:64, :], pst[3][0:64, :]))
                    s_num = P.tick("scalar", "sS", lambda e: e.copy(
                        num_sb[64:128, :], pst[4][64:128, :]))
                    P.wait("vector", "sS", s_num)
                    P.wait("vector", "tS", t_rep)
                    yv = P.tick("vector", "vS", lambda e, h=hp, c=cg: e.tensor_tensor(
                        yT(h)[:, c * 512:(c + 1) * 512], num_sb, pst[5][:, :], ALU.mult))
                    P.wait("tensor", "sS", s_num)
                    P.wait("tensor", "vS", yv)
            if "y0" in taps:
                tap_sb("y0", A.bf(g_yT, FC * T))
            t_p2_end = P.cnt["tS"]
            A.release(mk2)
            P.barrier()

        if stop_phase >= 3:
            mk = A.mark()
            o_lg = A.alloc(32)
            lg = A.f32(o_lg, 8)
            pn = A.f32(A.alloc(32), 8)
            ind = A.f32(A.alloc(32), 8)
            m8 = A.f32(A.alloc(32), 8)
            nmx = A.f32(A.alloc(8), 1)
            smx = A.f32(A.alloc(8), 1)
            wtok = A.bf(A.alloc(16), 8)
            smxr = A.f32(A.alloc(8), 1)
            spacer8 = A.f32(A.alloc(32), 8)
            P.wait("tensor", rwt_done)
            P.wait("tensor", "vS", P.cnt["vS"])
            sc_last = 0
            for tc in range(16):
                for fc in range(FC):
                    P.em("tensor", lambda e, f=fc, t=tc: e.matmul(
                        pst[6][:, 8 * t:8 * t + 8],
                        xn(f)[:, t * 128:(t + 1) * 128],
                        rwt_sb[:, f * E:(f + 1) * E],
                        start=(f == 0), stop=False, skip_group_check=True))
                P.em("tensor", lambda e: e.drain())
                mmb = lambda e, t=tc: e.matmul(
                    pst[6][:, 8 * t:8 * t + 8], sel1b[:], rbias_sb[:],
                    start=False, stop=True, skip_group_check=True)
                if tc == 15:
                    t_rl = P.tick("tensor", "tS", mmb)
                else:
                    P.em("tensor", mmb)
            P.wait("vector", "tS", t_rl)
            P.wait("scalar", "tS", t_rl)
            wtok_free = 0
            tr_done = []
            for tc in range(16):
                vv = P.tick("vector", "vS", lambda e, t=tc: e.reduce_max(
                    nmx, pst[6][:, 8 * t:8 * t + 8], axis=AX.X, negate=True))
                P.wait("scalar", "vS", vv)
                sv = P.tick("scalar", "sS", lambda e, t=tc: e.activation(
                    lg, pst[6][:, 8 * t:8 * t + 8], AF.Exp, bias=nmx, accum_out=smx))
                P.wait("vector", "sS", sv)
                vv = P.tick("vector", "vS", lambda e: e.reciprocal(smxr, smx))
                P.wait("scalar", "vS", vv)
                sv = P.tick("scalar", "sS", lambda e: e.activation(
                    pn, lg, AF.Copy, scale=smxr))
                P.wait("vector", "sS", sv)
                P.em("vector", lambda e: e.max(out=m8, in_=pn))
                P.em("vector", lambda e: e.tensor_copy(spacer8, pn))
                P.em("vector", lambda e: e.tensor_single_scalar(ind, pn, m8[0:128, 1:2], ALU.is_ge))
                P.em("vector", lambda e: e.tensor_copy(spacer8, pn))
                P.wait("vector", "tS", wtok_free)
                vv = P.tick("vector", "vS", lambda e: e.tensor_tensor(wtok, pn, ind, ALU.mult))
                P.wait("tensor", "vS", vv)
                tv = P.tick("tensor", "tS", lambda e, t=tc: e.transpose(
                    pbf[7][0:8, (t % 8) * 128:(t % 8) * 128 + 128], wtok, ident[:]))
                wtok_free = tv
                tr_done.append(tv)
                if tc % 8 == 7:
                    P.wait("scalar", "tS", tv)
                    sc_last = P.tick("scalar", "sS", lambda e, t=tc: e.copy(
                        wT_sb[0:8, (t - 7) * 128:(t + 1) * 128], pbf[7][0:8, 0:1024]))
                    P.wait("tensor", "sS", sc_last)
            tap_sb("r_lg", lg); tap_sb("r_pn", pn); tap_sb("r_ind", ind)
            tap_sb("r_m8", m8); tap_sb("r_smx", smx); tap_sb("r_nmx", nmx)
            tap_sb("r_wtok", wtok)
            A.release(mk)
            if "wtop" in taps:
                tap_sb("wtop", wT_sb[:])
            P.barrier()

            mk = A.mark()
            osb = [A.f32(A.alloc(1024), 512) for _ in range(2)]
            osb_free = [("a0", 0), ("a1", 0)]
            P.wait("tensor", wo_done)
            for oc in range(FC):
                for tt in range(4):
                    bk = (oc * 4 + tt) % 2
                    for fc in range(FC):
                        P.em("tensor", lambda e, f=fc, o=oc, t=tt, b=bk: e.matmul(
                            pst[b][:, :], wo_sb[:, f * D + o * 128: f * D + o * 128 + 128],
                            yT(f)[:, t * 512:(t + 1) * 512],
                            start=(f == 0), stop=False, skip_group_check=True))
                    P.em("tensor", lambda e: e.drain())
                    P.em("tensor", lambda e, o=oc, t=tt, b=bk: e.matmul(
                        pst[b][:, :], rb2_sb[0:8, o * 128:o * 128 + 128],
                        wT_sb[0:8, t * 512:(t + 1) * 512],
                        start=False, stop=False, skip_group_check=True))
                    tv = P.tick("tensor", "tS", lambda e, o=oc, b=bk: e.matmul(
                        pst[b][:, :], sb2_sb[0:1, o * 128:o * 128 + 128],
                        onesrow[:], start=False, stop=True, skip_group_check=True))
                    P.wait("scalar", "tS", tv)
                    P.wait("scalar", osb_free[bk])
                    sv = P.tick("scalar", "sS", lambda e, b=bk: e.copy(osb[b], pst[b][:, :]))
                    P.wait("tensor", "sS", sv)
                    P.wait("gpsimd", "sS", sv)
                    osb_free[bk] = P.dma(
                        "gpsimd", xscr[oc * 128:(oc + 1) * 128, tt * 512:(tt + 1) * 512],
                        osb[bk], "a" + str(bk), accum=True)
            t_p3_end = P.cnt["tS"]
            A.release(mk)
            tap_dram("xscr3", xscr)
            if "x2snap" in taps:
                mkS = A.mark()
                snapb = A.f32(A.alloc(4096), 2048)
                for fc in range(FC):
                    lvs = P.dma("sync", snapb, xscr[fc * 128:(fc + 1) * 128, :], "x")
                    P.wait("sync", lvs)
                    svs = P.dma("sync", x2snap[fc * 128:(fc + 1) * 128, :], snapb, "st")
                    P.wait("sync", svs)
                A.release(mkS)
                tap_dram("x2snap", x2snap)
            P.barrier()

        if stop_phase >= 4:
            A.release(mkL0)
            mk = A.mark()
            o_wb = [A.alloc(16384) for _ in range(3)]
            wb = [A.bf(o, 16384) for o in o_wb]
            o_hb = A.alloc(FC * 1024)
            hbuf = A.bf(o_hb, FC * 1024)
            hg = A.bf(A.alloc(1024), 512)
            hg2 = A.bf(A.alloc(1024), 512)
            dsum = [A.f32(A.alloc(1024), 512) for _ in range(4)]
            dsum_free = [("a0", 0), ("a1", 0), ("a2", 0), ("a3", 0)]
            bu = lambda k: [0, 2, 1][k % 3]
            bd = lambda k: [1, 0, 2][k % 3]
            up_done = [None] * 9
            dn_done = [None] * 9
            P.wait("gpsimd", "tS", t_p2_end)
            up_done[0] = blk_cast(lambda b: wb[0][:, b * HM:(b + 1) * HM], dr["sw1"],
                                  FC, HM, P.next_wsem())
            P.wait("gpsimd", "tS", t_p3_end)
            dn_done[0] = blk_cast(lambda b: wb[1][:, b * D:(b + 1) * D], dr["sw2"],
                                  16, D, P.next_wsem())
            up_done[1] = blk_cast(lambda b: wb[2][:, b * HM:(b + 1) * HM], dr["rw1"][0],
                                  FC, HM, P.next_wsem())
            accum_prev = None
            for k in range(9):
                ubuf, dbuf = wb[bu(k)], wb[bd(k)]
                shared = (k == 0)
                e_ = k - 1
                P.wait("tensor", up_done[k])
                wrep_free = 0
                for tt in range(4):
                    tsl = slice(tt * 512, (tt + 1) * 512)
                    t_wrep = 0
                    if not shared:
                        P.wait("tensor", "vS", wrep_free)
                        t_wrep = P.tick("tensor", "tS", lambda e, t=tt, x=e_: e.matmul(
                            pst[2][:, :], sel8[0:8, x * 128:(x + 1) * 128],
                            wT_sb[0:8, t * 512:(t + 1) * 512],
                            start=True, stop=True, skip_group_check=True))
                    ev_last = 0
                    ev_bank = [None, None]
                    hg_free = [0, 0]
                    for hc in range(16):
                        bk = hc % 2
                        if ev_bank[bk] is not None:
                            P.wait("tensor", *ev_bank[bk])
                        tv = 0
                        for fc in range(FC):
                            mm = lambda e, f=fc, h=hc, t=tsl, b=bk, u=ubuf: e.matmul(
                                pst[b][:, :], u[:, f * HM + h * 128: f * HM + h * 128 + 128],
                                xn(f)[:, t], start=(f == 0), stop=(f == FC - 1),
                                skip_group_check=True)
                            if fc == FC - 1:
                                tv = P.tick("tensor", "tS", mm)
                            else:
                                P.em("tensor", mm)
                        P.wait("scalar", "tS", tv)
                        bias_ap = (b_sb1[:, hc:hc + 1] if shared
                                   else b_rb1[:, e_ * 16 + hc: e_ * 16 + hc + 1])
                        if shared:
                            ev_last = P.tick("scalar", "sS", lambda e, h=hc, b=bk, ba=bias_ap:
                                e.activation(hbuf[:, h * 512:(h + 1) * 512], pst[b][:, :],
                                             AF.Gelu, bias=ba))
                            ev_bank[bk] = ("sS", ev_last)
                        else:
                            g_ = hg if bk == 0 else hg2
                            P.wait("scalar", "vS", hg_free[bk])
                            sv = P.tick("scalar", "sS", lambda e, b=bk, ba=bias_ap, g=g_:
                                e.activation(g, pst[b][:, :], AF.Gelu, bias=ba))
                            P.wait("vector", "sS", sv)
                            P.wait("vector", "tS", t_wrep)
                            ev_last = P.tick("vector", "vS", lambda e, h=hc, g=g_:
                                e.tensor_tensor(hbuf[:, h * 512:(h + 1) * 512], g,
                                                pst[2][:, :], ALU.mult))
                            hg_free[bk] = ev_last
                            ev_bank[bk] = ("sS", sv)
                    wrep_free = ev_last if not shared else 0
                    if tt == 3:
                        if k < 8:
                            P.wait("gpsimd", "tS", tv)
                            dn_done[k + 1] = blk_cast(
                                lambda b, bb=bu(k): wb[bb][:, b * D:(b + 1) * D],
                                dr["rw2"][k], 16, D, P.next_wsem())
                    P.wait("tensor", "sS" if shared else "vS", ev_last)
                    P.wait("tensor", dn_done[k])
                    if tt == 0 and accum_prev is not None:
                        for s_, v_ in accum_prev.items():
                            P.wait("gpsimd", s_, v_)
                    for dch in range(2):
                        tdn = 0
                        for hc in range(16):
                            for dc4 in range(4):
                                dc = dch * 4 + dc4
                                mm = lambda e, h=hc, d=dc, b=3 + dc4, t=tsl, db=dbuf: \
                                    e.matmul(pst[b][:, :],
                                             db[:, h * D + d * 128: h * D + d * 128 + 128],
                                             hbuf[:, h * 512:(h + 1) * 512],
                                             start=(h == 0), stop=(h == 15),
                                             skip_group_check=True)
                                if hc == 15 and dc4 == 3:
                                    tdn = P.tick("tensor", "tS", mm)
                                else:
                                    P.em("tensor", mm)
                        P.wait("vector", "tS", tdn)
                        for dc4 in range(4):
                            dc = dch * 4 + dc4
                            P.wait("vector", dsum_free[dc4])
                            vv = P.tick("vector", "vS", lambda e, b=3 + dc4, d=dc4:
                                e.tensor_copy(dsum[d], pst[b][:, :]))
                            P.wait("gpsimd", "vS", vv)
                            dsum_free[dc4] = P.dma(
                                "gpsimd",
                                xscr[dc * 128:(dc + 1) * 128, tsl], dsum[dc4],
                                "a" + str(dc4), accum=True)
                            P.wait("tensor", "vS", vv)
                accum_prev = {a: P.cnt[a] for a in ("a0", "a1", "a2", "a3")}
                if k < 7:
                    P.wait("gpsimd", "tS", tdn)
                    up_done[k + 2] = blk_cast(
                        lambda b, bb=bd(k): wb[bb][:, b * HM:(b + 1) * HM],
                        dr["rw1"][k + 1], FC, HM, P.next_wsem())
            A.release(mk)
            tap_dram("xscr4", xscr)
            P.barrier()

        if stop_phase >= 5:
            A.release(g_xn)
            g_xn2 = A.alloc(FC * T)
            xn2 = lambda fc: A.bf(g_xn2 + fc * T, T)
            emit_layernorm(xscr, xn2, store_xn=True, snap_dram=x1snap)
            tap_dram("x1snap", x1snap)
            if "xn1" in taps:
                tap_sb("xn1", A.bf(g_xn2, FC * T))
            P.barrier()

        if stop_phase >= 6:
            g_yT2 = A.alloc(FC * T)
            yT2 = lambda fc: A.bf(g_yT2 + fc * T, T)
            o_wa = [A.alloc(3072), A.alloc(3072)]
            wa_sb = [A.bf(o, 3072) for o in o_wa]
            o_wpj = A.alloc(FC * D)
            wpj_sb = A.bf(o_wpj, FC * D)
            mk6 = A.mark()
            qT6 = [[A.bf(A.alloc(T), T) for _ in range(2)] for _ in range(2)]
            kT6 = [[A.bf(A.alloc(T), T) for _ in range(2)] for _ in range(2)]
            vT6 = [A.bf(A.alloc(T), T) for _ in range(2)]
            vt6 = [[A.bf(A.alloc(1040), 1040) for _ in range(2)] for _ in range(2)]
            passA_scr = A.f32(A.alloc(1024), 512)
            mask1 = A.bf(A.alloc(4 * 512), 4 * 512)
            maskneg = A.f32(A.alloc(8 * 512), 4 * 512)
            mld = P.dma("sync", mask1.rearrange("p (r c) -> p r c", r=4),
                        c_mask1.rearrange("r p c -> p r c"), "c")
            mld = P.dma("sync", maskneg.rearrange("p (r c) -> p r c", r=4),
                        c_maskneg.rearrange("r p c -> p r c"), "c")
            P.wait("vector", mld)
            for par_ in range(2):
                for i_ in range(2):
                    P.em("vector", lambda e, p_=par_, j_=i_: e.memset(
                        kT6[p_][j_][64:65, :], 1.0))
            stg6 = [A.bf(A.alloc(512), 512) for _ in range(2)]
            stg6_gate = [None, None]
            o_pp = [A.alloc(512) for _ in range(4)]
            pp = [A.bf(o, 512) for o in o_pp]
            o_num6 = A.alloc(1024)
            num6 = A.f32(o_num6, 512)
            mq_col = A.f32(A.alloc(16), 1)
            mq_mat = A.f32(A.alloc(16), 4)
            spacer6 = A.f32(A.alloc(32), 8)
            mq_tmp = A.f32(A.alloc(16), 1)
            mqn_bf = A.bf(A.alloc(16), 1)
            mqn8 = A.bf(A.alloc(16), 8)

            def wa_dma(hp, slot):
                return P.dma("gpsimd",
                             wa_sb[slot][:].rearrange("p (b c) -> p b c", b=FC),
                             dr["wa"][hp].rearrange("b p c -> p b c"), P.next_wsem())
            wa_done = [None] * 10
            P.wait("gpsimd", "tS", P.cnt["tS"])
            P.wait("gpsimd", "vS", P.cnt["vS"])
            P.wait("gpsimd", "st", P.cnt["st"])
            wa_done[0] = wa_dma(0, 0)
            wa_done[1] = wa_dma(1, 1)
            wpj_done = blk_cast(lambda b: wpj_sb[:, b * D:(b + 1) * D], dr["wproj"],
                                FC, D, P.next_wsem())

            s1_state = [None, None]

            def make_stage1(hp):
                par = hp % 2
                slot = hp % 2
                info = {"proj": 0, "vt": 0, "mrow": 0, "tmq": 0}
                s1_state[par] = info
                bankp = [2]
                bank_gate = {2: None, 6: None}
                slot_ev = [0] * 8

                def nb():
                    b = bankp[0]
                    bankp[0] = 8 - b
                    return b

                def proj_chunk(part, tt):
                    def em():
                        b = nb()
                        if bank_gate[b] is not None:
                            for h_ in bank_gate[b]:
                                P.wait("tensor", *h_)
                        if part == 0 and tt == 0:
                            P.wait("tensor", wa_done[hp])
                        tv = 0
                        for fc in range(FC):
                            mm = lambda e, f=fc, b_=b, s=slot, pt=part, t=tt: e.matmul(
                                pst[b_][:, :],
                                wa_sb[s][:, f * 384 + pt * 128: f * 384 + pt * 128 + 128],
                                xn2(f)[:, t * 512:(t + 1) * 512],
                                start=(f == 0), stop=(f == FC - 1), skip_group_check=True)
                            if fc == FC - 1:
                                tv = P.tick("tensor", "tS", mm)
                            else:
                                P.em("tensor", mm)
                        P.wait("scalar", "tS", tv)
                        if part == 2:
                            sv = P.tick("scalar", "sS", lambda e, b_=b, t=tt: e.copy(
                                vT6[par][:, t * 512:(t + 1) * 512], pst[b_][:, :]))
                            bank_gate[b] = [("sS", sv)]
                        else:
                            d2 = (qT6, kT6)[part][par]
                            P.em("scalar", lambda e, b_=b, d=d2, t=tt: e.copy(
                                d[0][0:64, t * 512:(t + 1) * 512], pst[b_][0:64, :]))
                            sg = tt % 2
                            if stg6_gate[sg] is not None:
                                P.wait("scalar", *stg6_gate[sg])
                            sv = P.tick("scalar", "sS", lambda e, b_=b, g=sg: e.copy(
                                stg6[g][64:128, :], pst[b_][64:128, :]))
                            P.wait("sync", "sS", sv)
                            sem6 = ("x0", "x1")[tt % 2] if part == 0 else ("s0", "s1")[tt % 2]
                            dh = P.dma("sync",
                                       d2[1][0:64, tt * 512:(tt + 1) * 512],
                                       stg6[sg][64:128, :], sem6)
                            stg6_gate[sg] = dh
                            bank_gate[b] = [("sS", sv)]
                            info.setdefault("pdmas", {})[sem6] = dh
                        info["proj"] = sv
                        if part == 2 and tt == 3 and hp + 2 < FC:
                            P.wait("gpsimd", "tS", tv)
                            wa_done[hp + 2] = wa_dma(hp + 2, (hp + 2) % 2)
                    return em

                def passa_chunk(i, qc):
                    def em():
                        ndiag = qc // 4
                        P.wait("tensor", "sS", info["proj"])
                        if i == 1 and qc == 0:
                            for dh_ in info.get("pdmas", {}).values():
                                P.wait("tensor", *dh_)
                        for ktile in range(ndiag + 1):
                            b = nb()
                            if bank_gate[b] is not None:
                                for h_ in bank_gate[b]:
                                    P.wait("tensor", *h_)
                            tn = P.tick("tensor", "tS", lambda e, i_=i, q=qc, k=ktile, b_=b:
                                e.matmul(pst[b_][:, 0:512],
                                         qT6[par][i_][0:64, q * 128:(q + 1) * 128],
                                         kT6[par][i_][0:64, k * 512:(k + 1) * 512],
                                         start=True, stop=True, skip_group_check=True))
                            P.wait("vector", "tS", tn)
                            if ktile == ndiag:
                                P.em("vector", lambda e, b_=b, r=qc - 4 * ktile: e.tensor_tensor(
                                    passA_scr, pst[b_][:, 0:512],
                                    maskneg[:, r * 512:(r + 1) * 512], ALU.add))
                                vv = P.tick("vector", "vS", lambda e, k_=ktile: e.reduce_max(
                                    mq_mat[:, k_:k_ + 1], passA_scr, axis=AX.X))
                            else:
                                vv = P.tick("vector", "vS", lambda e, b_=b, k_=ktile: e.reduce_max(
                                    mq_mat[:, k_:k_ + 1], pst[b_][:, 0:512], axis=AX.X))
                            bank_gate[b] = [("vS", vv)]
                        if ndiag == 0:
                            vcast = P.tick("vector", "vS", lambda e, q8=qc % 8:
                                e.tensor_scalar_mul(mqn8[:, q8:q8 + 1], mq_mat[:, 0:1], -1.0))
                        else:
                            P.em("vector", lambda e: e.tensor_copy(spacer6, maskneg[:, 0:8]))
                            P.em("vector", lambda e, n_=ndiag: e.reduce_max(
                                mq_col[:], mq_mat[:, 0:n_ + 1], axis=AX.X))
                            P.em("vector", lambda e: e.tensor_copy(spacer6, maskneg[:, 0:8]))
                            vcast = P.tick("vector", "vS", lambda e, q8=qc % 8:
                                e.tensor_scalar_mul(mqn8[:, q8:q8 + 1], mq_col[:], -1.0))
                        if qc % 8 == 7:
                            P.wait("tensor", "vS", vcast)
                            P.wait("tensor", "sS", info["mrow"])
                            tmq = 0
                            for q8 in range(8):
                                tr = lambda e, q8_=q8: e.transpose(
                                    pbf[7][64:65, q8_ * 128:q8_ * 128 + 128],
                                    mqn8[:, q8_:q8_ + 1], ident[:, :])
                                if q8 == 7:
                                    tmq = P.tick("tensor", "tS", tr)
                                else:
                                    P.em("tensor", tr)
                            info["tmq"] = tmq
                            P.wait("scalar", "tS", tmq)
                            info["mrow"] = P.tick("scalar", "sS", lambda e, i_=i, q0=qc - 7:
                                e.copy(qT6[par][i_][64:65, q0 * 128:(q0 + 8) * 128],
                                       pbf[7][64:65, 0:1024]))
                            P.wait("vector", "tS", tmq)
                    return em

                def vtok_chunk(i):
                    def em():
                        P.wait("tensor", "sS", info["proj"])
                        P.wait("tensor", "sS", info["mrow"])
                        P.wait("tensor", "vS", info["vt"])
                        idsl = ident[64 * i:64 * i + 64, 64 * i:64 * i + 64]
                        tv = 0
                        for tc in range(16):
                            tr = lambda e, i_=i, t=tc, s=idsl: e.transpose(
                                pbf[7][:, t * 64:t * 64 + 64],
                                vT6[par][64 * i_:64 * i_ + 64, t * 128:(t + 1) * 128], s)
                            if tc == 15:
                                tv = P.tick("tensor", "tS", tr)
                            else:
                                P.em("tensor", tr)
                        P.wait("vector", "tS", tv)
                        vvt = P.tick("vector", "vS", lambda e, i_=i: e.tensor_copy(
                            vt6[par][i_][:].rearrange("p (t c) -> p t c", c=65)[:, :, 0:64],
                            pbf[7][:, 0:1024].rearrange("p (t c) -> p t c", c=64)))
                        P.em("vector", lambda e, i_=i: e.memset(
                            vt6[par][i_][:].rearrange("p (t c) -> p t c", c=65)[:, :, 64:65], 1.0))
                        info["vt"] = vvt
                    return em

                chunks = [proj_chunk(0, tt) for tt in range(4)]
                chunks += [proj_chunk(1, tt) for tt in range(4)]
                chunks += [passa_chunk(i, qc) for i in range(2) for qc in range(16)]
                chunks += [proj_chunk(2, tt) for tt in range(4)]
                chunks += [vtok_chunk(i) for i in range(2)]
                return chunks

            def stage2(hp, next_chunks):
                par = hp % 2
                info = s1_state[par]
                P.wait("tensor", "sS", info["mrow"])
                P.wait("tensor", "vS", info["vt"])
                for i in range(2):
                    for qt in range(4):
                        kmax = 4 * qt + 3
                        pp_free = [0, 0, 0, 0]
                        exp_done = [0] * 16
                        bks = (0, 1, 4 if i == 0 else 3)

                        def emit_out(c, ready):
                            if c == 0 and outbank_gate[0] is not None:
                                for w_ in outbank_gate[0]:
                                    P.wait("tensor", *w_)
                                outbank_gate[0] = None
                            P.wait("tensor", *ready)
                            sl3 = c % 4
                            if i == 0:
                                mo = lambda e, c_=c, s=sl3: e.matmul(
                                    pst[3][0:65, :], vt6[par][0][:, c_ * 65:c_ * 65 + 65],
                                    pp[s][:, :], start=(c_ == 0), stop=(c_ == kmax),
                                    skip_group_check=True)
                                pp_free[sl3] = P.tick("tensor", "tS", mo)
                            else:
                                P.em("tensor", lambda e, c_=c, s=sl3: e.matmul(
                                    pst[4][64:128, :], vt6[par][1][:, c_ * 65:c_ * 65 + 64],
                                    pp[s][:, :], start=(c_ == 0), stop=(c_ == kmax),
                                    skip_group_check=True))
                                pp_free[sl3] = P.tick("tensor", "tS",
                                    lambda e, c_=c, s=sl3: e.matmul(
                                        pst[4][0:1, :], vt6[par][1][:, c_ * 65 + 64:c_ * 65 + 65],
                                        pp[s][:, :], start=(c_ == 0), stop=(c_ == kmax),
                                        skip_group_check=True))

                        pending = []
                        for kc in range(kmax + 1):
                            if next_chunks:
                                next_chunks.pop(0)()
                            bk = bks[kc % 3]
                            if kc == 2 and qt == 0 and outbank_gate[0] is not None:
                                for w_ in outbank_gate[0]:
                                    P.wait("tensor", *w_)
                                outbank_gate[0] = None
                            if kc >= 3:
                                P.wait("tensor", "sS", exp_done[kc - 3])
                            ts_ = P.tick("tensor", "tS", lambda e, i_=i, q=qt, c=kc, b=bk:
                                e.matmul(pst[b][:, 0:512],
                                         kT6[par][i_][0:65, c * 128:(c + 1) * 128],
                                         qT6[par][i_][0:65, q * 512:(q + 1) * 512],
                                         start=True, stop=True, skip_group_check=True))
                            if len(pending) >= 3:
                                emit_out(*pending.pop(0))
                            sl3 = kc % 4
                            P.wait("scalar", "tS", ts_)
                            if pp_free[sl3]:
                                P.wait("scalar", "tS", pp_free[sl3])
                            exp_done[kc] = P.tick("scalar", "sS", lambda e, b=bk, s=sl3:
                                e.activation(pp[s][:, :], pst[b][:, :], AF.Exp,
                                             scale=0.125))
                            ready = ("sS", exp_done[kc])
                            r = kc - 4 * qt
                            if r >= 0:
                                P.wait("vector", "sS", exp_done[kc])
                                vmask = P.tick("vector", "vS", lambda e, s=sl3, r_=r:
                                    e.tensor_tensor(pp[s][:, :], pp[s][:, :],
                                                    mask1[:, r_ * 512:(r_ + 1) * 512],
                                                    ALU.mult))
                                ready = ("vS", vmask)
                            pending.append((kc, ready))
                        if next_chunks:
                            next_chunks.pop(0)()
                        for pe_ in pending:
                            emit_out(*pe_)
                        pending = []
                        t_out = pp_free[kmax % 4]
                        if next_chunks:
                            next_chunks.pop(0)()
                        P.wait("vector", "tS", t_out)
                        if i == 0:
                            P.em("vector", lambda e: e.tensor_single_scalar(
                                dr_row[:], pst[3][64:65, :], 1e-38, ALU.max))
                            v_dr = P.tick("vector", "vS", lambda e: e.reciprocal(
                                dr_row[:], dr_row[:]))
                        else:
                            P.em("vector", lambda e: e.tensor_single_scalar(
                                dr_row[:], pst[4][0:1, :], 1e-38, ALU.max))
                            v_dr = P.tick("vector", "vS", lambda e: e.reciprocal(
                                dr_row[:], dr_row[:]))
                        P.wait("tensor", "vS", v_dr)
                        t_rep = P.tick("tensor", "tS", lambda e, i_=i: e.matmul(
                            pst[5][64 * i_:64 * i_ + 64, :], sel1f[0:1, 0:64],
                            dr_row[:], start=True, stop=True, skip_group_check=True))
                        P.wait("scalar", "tS", t_out)
                        P.wait("scalar", "vS", yv_prev[0])
                        if i == 0:
                            s_num = P.tick("scalar", "sS", lambda e: e.copy(
                                num6[0:64, :], pst[3][0:64, :]))
                        else:
                            s_num = P.tick("scalar", "sS", lambda e: e.copy(
                                num6[64:128, :], pst[4][64:128, :]))
                        P.wait("vector", "sS", s_num)
                        P.wait("vector", "tS", t_rep)
                        yv = P.tick("vector", "vS", lambda e, h=hp, q=qt, i_=i: e.tensor_tensor(
                            yT2(h)[64 * i_:64 * i_ + 64, q * 512:(q + 1) * 512],
                            num6[64 * i_:64 * i_ + 64, :],
                            pst[5][64 * i_:64 * i_ + 64, :], ALU.mult))
                        outbank_gate[0] = [("sS", s_num), ("vS", yv)]
                        yv_prev[0] = yv

            outbank_gate = [None]
            yv_prev = [0]
            cur = make_stage1(0)
            for ch in cur:
                ch()
            for hp in range(FC):
                nxt = make_stage1(hp + 1) if hp + 1 < FC else []
                stage2(hp, nxt)
                for ch in nxt:
                    ch()

            if "y1" in taps:
                tap_sb("y1", A.bf(g_yT2, FC * T))
            tap_sb("q6", qT6[0][0])
            P.barrier()
            A.release(mk6)

            mk7 = A.mark()
            osb7 = [A.f32(A.alloc(1024), 512) for _ in range(4)]
            raw7 = [A.f32(A.alloc(2048), 1024) for _ in range(4)]
            g_sems = ("x0", "x1", "s0", "s1")
            red_prev7 = [0, 0, 0, 0]

            def g_chunk(c):
                def em():
                    sl4 = c % 4
                    P.wait("sync", "vS", red_prev7[sl4])
                    if c < 32:
                        src = dr["w1t"][c // 4][:, (c % 4) * 1024:(c % 4 + 1) * 1024]
                    else:
                        src = dr["w2t"][c - 32]
                    ld = P.dma("sync", raw7[sl4], src, g_sems[sl4])
                    P.wait("vector", ld)
                    red_prev7[sl4] = P.tick("vector", "vS", lambda e, s_=sl4, c_=c:
                        e.reduce_sum(gp2[:, c_:c_ + 1], raw7[s_], axis=AX.X,
                                     apply_absolute_value=True))
                return em
            gch = [g_chunk(c) for c in range(64)]
            osb7_free = [("a0", 0), ("a1", 0), ("a2", 0), ("a3", 0)]
            bank7_free = [None, None, None, None]
            P.wait("tensor", wpj_done)
            for oc in range(FC):
                for tt in range(4):
                    k4 = (oc * 4 + tt) % 4
                    bk = k4
                    if bank7_free[k4] is not None:
                        P.wait("tensor", *bank7_free[k4])
                    tv = 0
                    for fc in range(FC):
                        mm = lambda e, f=fc, o=oc, t=tt, b=bk: e.matmul(
                            pst[b][:, :], wpj_sb[:, f * D + o * 128: f * D + o * 128 + 128],
                            yT2(f)[:, t * 512:(t + 1) * 512],
                            start=(f == 0), stop=(f == FC - 1), skip_group_check=True)
                        if fc == FC - 1:
                            tv = P.tick("tensor", "tS", mm)
                        else:
                            P.em("tensor", mm)
                    P.wait("scalar", "tS", tv)
                    P.wait("scalar", osb7_free[k4])
                    sv = P.tick("scalar", "sS", lambda e, b=bk, k=k4: e.copy(osb7[k], pst[b][:, :]))
                    bank7_free[k4] = ("sS", sv)
                    P.wait("gpsimd", "sS", sv)
                    osb7_free[k4] = P.dma(
                        "gpsimd", xscr[oc * 128:(oc + 1) * 128, tt * 512:(tt + 1) * 512],
                        osb7[k4], "a" + str(k4), accum=True)
                    for _ in range(2):
                        if gch:
                            gch.pop(0)()
            for ch_ in gch:
                ch_()
            A.release(mk7)
            tap_dram("xscr7", xscr)
            tap_dram("xnscr", xnscr)
            P.barrier()

        if stop_phase >= 8:
            A.release(g_xn2)

            def thr_from(gcols, dst, after=None):
                if after is not None:
                    P.wait("tensor", "sS", after)
                tg_ = P.tick("tensor", "tS", lambda e, g=gcols: e.matmul(
                    pst[2][0:1, 0:32], onescol[:], g, start=True, stop=True,
                    skip_group_check=True))
                P.wait("vector", "tS", tg_)
                vg_ = P.tick("vector", "vS", lambda e: e.reduce_sum(
                    t1_row[0:1, 0:1], pst[2][0:1, 0:32], axis=AX.X))
                P.wait("tensor", "vS", vg_)
                tg2 = P.tick("tensor", "tS", lambda e: e.matmul(
                    pst[2][0:128, 0:1], sel1f[:], t1_row[0:1, 0:1],
                    start=True, stop=True, skip_group_check=True))
                P.wait("scalar", "tS", tg2)
                return P.tick("scalar", "sS", lambda e, d=dst: e.activation(
                    d[:], pst[2][:, 0:1], AF.Copy, scale=0.5 / 4194304.0, bias=0.5e-5))

            thr1_done = thr_from(gp2[:, 0:32], thr_sb)
            thr2_done = thr_from(gp2[:, 32:64], thr2_sb, after=thr1_done)
            o_w1q = A.alloc(FC * DFF)
            w1q = A.bf(o_w1q, FC * DFF)
            o_w2q = A.alloc(32 * D)
            w2q = A.bf(o_w2q, 32 * D)
            mkq = A.mark()
            r1 = [A.f32(A.alloc(4096), 2048) for _ in range(2)]
            g1 = [A.bf(A.alloc(2048), 2048) for _ in range(2)]
            r2 = [A.f32(A.alloc(2048), 1024) for _ in range(2)]
            g2 = [A.bf(A.alloc(1024), 1024) for _ in range(2)]
            sq1 = [0, 0]
            sq2 = [0, 0]
            P.wait("vector", "sS", thr2_done)

            def q2_chunk(mat, b):
                sl2 = b % 2
                if mat == 0:
                    raw, sgn, thr = r1[sl2], g1[sl2], thr_sb
                    src = dr["w1t"][b // 2][:, (b % 2) * 2048:(b % 2 + 1) * 2048]
                    dst = w1q[:, b * 2048:(b + 1) * 2048]
                    sems, prev = ("x0", "x1"), sq1
                else:
                    raw, sgn, thr = r2[sl2], g2[sl2], thr2_sb
                    src = dr["w2t"][b]
                    dst = w2q[:, b * 1024:(b + 1) * 1024]
                    sems, prev = ("s0", "s1"), sq2
                P.wait("sync", "vS", prev[sl2])
                ld = P.dma("sync", raw, src, sems[sl2])
                P.wait("scalar", ld)
                P.tick("scalar", "sS", lambda e: e.sign(sgn, raw))
                s2_ = P.tick("scalar", "sS", lambda e: e.activation(raw, raw, AF.Abs))
                P.wait("vector", "sS", s2_)
                P.em("vector", lambda e: e.tensor_single_scalar(
                    dst, raw, thr[:], ALU.is_gt))
                prev[sl2] = P.tick("vector", "vS", lambda e: e.tensor_tensor(
                    dst, dst, sgn, ALU.mult))

            nb1, nb2 = 0, 0
            for trio in range(16):
                q2_chunk(1, nb2); nb2 += 1
                q2_chunk(0, nb1); nb1 += 1
                q2_chunk(1, nb2); nb2 += 1
            q1_done = P.cnt["vS"]
            q2_done = q1_done
            A.release(mkq)
            tap_sb("thr1", thr_sb[:])
            tap_sb("gp1", gp2)
            tap_sb("w1q", A.bf(o_w1q, FC * DFF))
            o_hb8 = A.alloc(32 * 256)
            hb8 = A.bf(o_hb8, 32 * 256)
            o_xnr = [A.alloc(256) for _ in range(2 * FC)]
            xnr = [A.bf(o, 256) for o in o_xnr]
            o_xt8 = [A.alloc(512) for _ in range(8)]
            xt8 = [A.f32(o, 256) for o in o_xt8]
            o_ot8 = [A.alloc(512) for _ in range(4)]
            ot8 = [A.f32(o, 256) for o in o_ot8]
            P.wait("tensor", "vS", q1_done)
            P.wait("tensor", "vS", q2_done)
            xt_free = [0] * 8
            xld_sem = [("x0", 0), ("x1", 0)]
            st_done = [("s0", 0)] * 4
            up_done8 = [0, 0]
            for tg in range(8):
                tsl = slice(tg * 256, (tg + 1) * 256)
                tp8 = tg % 2
                P.wait("sync", "tS", up_done8[tp8])
                lds8 = None
                for fc in range(FC):
                    lds8 = P.dma("sync", xnr[tp8 * FC + fc],
                                 xnscr[fc * 128:(fc + 1) * 128, tsl], "x")
                if tg == 0:
                    P.wait("sync", "vS", q2_done)
                for dc in range(8):
                    P.wait("sync", "vS", xt_free[dc])
                    xld_sem[dc % 2] = P.dma(
                        "sync", xt8[dc], xscr[dc * 128:(dc + 1) * 128, tsl],
                        "x" + str(dc % 2))
                P.wait("tensor", lds8)
                ev_bank = [None, None]
                ev_last = 0
                for hc in range(32):
                    bk = hc % 2
                    if ev_bank[bk] is not None:
                        P.wait("tensor", *ev_bank[bk])
                    tv = 0
                    for fc in range(FC):
                        mm = lambda e, f=fc, h=hc, b=bk, p8=tp8: e.matmul(
                            pst[b][:, 0:256], w1q[:, f * DFF + h * 128: f * DFF + h * 128 + 128],
                            xnr[p8 * FC + f], start=(f == 0), stop=(f == FC - 1),
                            skip_group_check=True)
                        if fc == FC - 1:
                            tv = P.tick("tensor", "tS", mm)
                        else:
                            P.em("tensor", mm)
                    P.wait("scalar", "tS", tv)
                    ev_last = P.tick("scalar", "sS", lambda e, h=hc, b=bk: e.activation(
                        hb8[:, h * 256:(h + 1) * 256], pst[b][:, 0:256], AF.Relu,
                        bias=b_b1[:, h:h + 1]))
                    ev_bank[bk] = ("sS", ev_last)
                up_done8[tp8] = tv
                P.wait("tensor", "sS", ev_last)
                for dch in range(2):
                    tdn = 0
                    for hc in range(32):
                        for dc4 in range(4):
                            dc = dch * 4 + dc4
                            mm = lambda e, h=hc, d=dc, b=2 + dc4: e.matmul(
                                pst[b][:, 0:256], w2q[:, h * D + d * 128: h * D + d * 128 + 128],
                                hb8[:, h * 256:(h + 1) * 256],
                                start=(h == 0), stop=(h == 31), skip_group_check=True)
                            if hc == 31 and dc4 == 3:
                                tdn = P.tick("tensor", "tS", mm)
                            else:
                                P.em("tensor", mm)
                    for dc4 in range(4):
                        dc = dch * 4 + dc4
                        ox = dc % 4
                        if dc == 0:
                            P.wait("vector", xld_sem[0])
                            P.wait("vector", xld_sem[1])
                        P.wait("vector", st_done[ox])
                        P.wait("vector", "tS", tdn)
                        vv = P.tick("vector", "vS", lambda e, b=2 + dc4, d=dc, s=ox:
                            e.scalar_tensor_tensor(ot8[s], pst[b][:, 0:256],
                                                   b_b2[:, d:d + 1], xt8[d],
                                                   ALU.add, ALU.add))
                        xt_free[dc] = vv
                        P.wait("sync", "vS", vv)
                        st_done[ox] = P.dma("sync", out[dc * 128:(dc + 1) * 128, tsl],
                                            ot8[ox], ("s0", "s1", "a0", "a1")[ox])
                        P.wait("tensor", "vS", vv)

        for name, (dram_ap, src_ap, kind) in dbg.items():
            if kind == "sb":
                P.dma("sync", dram_ap, src_ap, "st")
        mkb = A.mark()
        if stop_phase >= 8:
            bounce_f = A.f32(o_hb8, 2048)
            bounce_b = A.bf(o_hb8 + 4096, 2048)
        else:
            bounce_f = A.f32(A.alloc(4096), 2048)
            bounce_b = A.bf(A.alloc(2048), 2048)
        for name, (dram_ap, src_ap, kind) in dbg.items():
            if kind == "dram":
                bounce = bounce_f if src_ap.dtype == F32 else bounce_b
                for fc in range(src_ap.shape[0] // 128):
                    lv = P.dma("sync", bounce, src_ap[fc * 128:(fc + 1) * 128, :], "x")
                    P.wait("sync", lv)
                    sv_ = P.dma("sync", dram_ap[fc * 128:(fc + 1) * 128, :], bounce, "st")
                    P.wait("sync", sv_)
        A.release(mkb)
        for s_ in ("c", "x", "x0", "x1", "st", "s0", "s1"):
            P.wait("sync", s_, P.cnt[s_])
        for s_ in ("w0", "w1", "w2", "w3", "a0", "a1", "a2", "a3"):
            P.wait("gpsimd", s_, P.cnt[s_])

        with nc.Block() as block:
            for eng in ENGINES:
                def runner(e, eng=eng):
                    for f in P.streams[eng]:
                        f(e)
                getattr(block, eng)(runner)
    return nc


def host_inputs(inp, core):
    x = inp["L0_attn_x"]
    d = {
        "xin": np.ascontiguousarray(np.asarray(x[core]).T),
        "wq": _blk(np.asarray(inp["L0_attn_W_q"]).T),
        "wk": _blk(np.asarray(inp["L0_attn_W_k"]).T),
        "wv": _blk(np.asarray(inp["L0_attn_W_v"]).T),
        "wo": _blk(np.asarray(inp["L0_attn_W_o"]).T),
        "rwt": _blk(np.asarray(inp["L0_ffn_router_w"])),
        "rbias": np.ascontiguousarray(np.asarray(inp["L0_ffn_router_b"]).reshape(1, E)),
        "sw1": _blk(np.asarray(inp["L0_ffn_sw1"])),
        "sw2": _blk(np.asarray(inp["L0_ffn_sw2"])),
        "rw1": np.ascontiguousarray(np.asarray(inp["L0_ffn_rw1"]).reshape(E, FC, 128, HM)),
        "rw2": np.ascontiguousarray(np.asarray(inp["L0_ffn_rw2"]).reshape(E, HM // 128, 128, D)),
        "rb1": np.ascontiguousarray(np.asarray(inp["L0_ffn_rb1"]).reshape(E, 16, 128)
                                    .transpose(2, 0, 1).reshape(128, E * 16)),
        "rb2": np.ascontiguousarray(np.asarray(inp["L0_ffn_rb2"])),
        "sb1": np.ascontiguousarray(np.asarray(inp["L0_ffn_sb1"]).reshape(16, 128).T),
        "sb2": np.ascontiguousarray(np.asarray(inp["L0_ffn_sb2"]).reshape(1, D)),
        "w1t": _blk(np.asarray(inp["L1_ffn_W1"]).T),
        "w2t": _blk(np.asarray(inp["L1_ffn_W2"]).T),
        "b1": np.ascontiguousarray(np.asarray(inp["L1_ffn_b1"]).reshape(32, 128).T),
        "b2": np.ascontiguousarray(np.asarray(inp["L1_ffn_b2"]).reshape(8, 128).T),
        "wproj": _blk(np.asarray(inp["L1_attn_w_proj"]).T),
    }
    wat = np.asarray(inp["L1_attn_w_attn"]).T
    wa = np.empty((FC, FC, 128, 384), np.float32)
    for hp in range(FC):
        cols = np.concatenate([
            wat[:, 128 * hp:128 * hp + 128],
            wat[:, 1024 + 128 * hp:1024 + 128 * hp + 128],
            wat[:, 2048 + 128 * hp:2048 + 128 * hp + 128]], axis=1)
        wa[hp] = cols.reshape(FC, 128, 384)
    d["wa"] = np.ascontiguousarray(wa)
    return d


def kernel(**inputs):
    nc = build()
    in_maps = [host_inputs(inputs, c) for c in range(NCORES)]
    res = run_bass_kernel_spmd(nc, in_maps, core_ids=list(range(NCORES)))
    return np.stack([np.asarray(res.results[c]["out"]).T for c in range(NCORES)])
